# version 01
# baseline (speedup 1.0000x reference)
import math

import numpy as np
import ml_dtypes

import concourse.bass as bass
import concourse.mybir as mybir
from concourse.bass_utils import run_bass_kernel_spmd

F32 = mybir.dt.float32
BF16 = mybir.dt.bfloat16
U32 = mybir.dt.uint32
AF = mybir.ActivationFunctionType
ALU = mybir.AluOpType
AX = mybir.AxisListType

N_CORES = 8
D = 2048
FC = 16
T_FULL = 8192
T_OWN = 1024
HALO = 256
G = 16
N_HEADS = 16
HDIM = 128
WINDOW = 256
LN_EPS = 1e-5
D_HID = 8192
NEG = -1e30
SCALE_ATT = 1.0 / math.sqrt(HDIM)
N_EXP = 8
F_EXP = 1408
FS = 2816
TW = HALO + T_OWN
TK = TW + G

BFNP = ml_dtypes.bfloat16



STREAM_SECTION = {"pe": "tensor", "dve": "vector", "act": "scalar",
                  "pool": "gpsimd", "sp": "sync"}


def _dsize(dt):
    return {F32: 4, BF16: 2, U32: 4}[dt]


class _ViewH:

    def __init__(self, buf):
        self.b = buf

    def _base(self):
        b = self.b
        e0 = b.off // 2
        ne = (b.shape[1] * _dsize(b.dtype)) // 2
        ap = b.arena.h[:, e0:e0 + ne]
        if b.dtype != BF16:
            ap = ap.bitcast(b.dtype)
        return ap

    def __getitem__(self, sl):
        return self._base()[sl]

    def ap(self):
        return self._base()


class Buf:
    __slots__ = ("name", "shape", "dtype", "space", "kind", "h", "arena", "off")

    def __init__(self, name, shape, dtype, space, kind=None, arena=None, off=0):
        self.name, self.shape, self.dtype = name, tuple(shape), dtype
        self.space, self.kind = space, kind
        self.arena, self.off = arena, off
        self.h = None

    def key(self):
        return self.arena.name if self.arena is not None else self.name

    def byte_ivl(self, lo, hi):
        ds = _dsize(self.dtype) if self.space != "dram" else 1
        if self.space == "dram":
            return (0, 1 << 58) if hi >= (1 << 57) else (lo, hi)
        hi = min(hi, self.shape[-1])
        return (self.off + lo * ds, self.off + hi * ds)

    def __repr__(self):
        return f"Buf({self.name})"


class Prog:
    def __init__(self, name=""):
        self.name = name
        self.bufs = {}
        self.ops = []
        self.lanes = {}
        self.acc = {}

    def _reg(self, b):
        self.bufs[b.name] = b
        self.acc.setdefault(b.key(), {"w": [], "r": []})
        return b

    def sbuf(self, name, p, f, dtype):
        return self._reg(Buf(name, (p, f), dtype, "sbuf"))

    def view(self, name, p, f, dtype, arena, off):
        assert off % 64 == 0
        assert off + f * _dsize(dtype) <= arena.shape[1] * 2, name
        b = Buf(name, (p, f), dtype, "sbuf", arena=arena, off=off)
        b.h = _ViewH(b)
        return self._reg(b)

    def psum(self, name, p, f, dtype=F32):
        return self._reg(Buf(name, (p, f), dtype, "psum"))

    def dram(self, name, shape, dtype, kind):
        return self._reg(Buf(name, tuple(shape), dtype, "dram", kind))

    def emit(self, stream, fn, reads=(), writes=(), dma=False, tag=""):
        lane = (stream, "d" if dma else "c")
        idx = len(self.ops)
        deps = set()

        def _ivl(spec):
            if isinstance(spec, Buf):
                return (spec.key(),) + spec.byte_ivl(0, 1 << 58)
            b, lo, hi = spec
            return (b.key(),) + b.byte_ivl(lo, hi)

        for spec in reads:
            nm, lo, hi = _ivl(spec)
            for (wl, wh, wi) in self.acc[nm]["w"]:
                if wl < hi and lo < wh:
                    deps.add(wi)
        for spec in writes:
            nm, lo, hi = _ivl(spec)
            a = self.acc[nm]
            for (wl, wh, wi) in a["w"]:
                if wl < hi and lo < wh:
                    deps.add(wi)
            keep = []
            for (rl, rh, ri) in a["r"]:
                if rl < hi and lo < rh:
                    deps.add(ri)
                else:
                    keep.append((rl, rh, ri))
            a["r"] = keep
        for spec in reads:
            nm, lo, hi = _ivl(spec)
            self.acc[nm]["r"].append((lo, hi, idx))
        for spec in writes:
            nm, lo, hi = _ivl(spec)
            a = self.acc[nm]
            neww = []
            for (wl, wh, wi) in a["w"]:
                if wl < hi and lo < wh:
                    if wl < lo:
                        neww.append((wl, lo, wi))
                    if hi < wh:
                        neww.append((hi, wh, wi))
                else:
                    neww.append((wl, wh, wi))
            neww.append((lo, hi, idx))
            a["w"] = neww

        deps.discard(idx)
        lane_ops = self.lanes.setdefault(lane, [])
        op = dict(idx=idx, lane=lane, ord=len(lane_ops), fn=fn, deps=deps,
                  inc=False, tag=tag)
        self.ops.append(op)
        lane_ops.append(idx)
        if dma and op["ord"] >= 8:
            deps.add(lane_ops[op["ord"] - 8])
        for d in list(deps):
            dop = self.ops[d]
            if dop["lane"] == lane and stream == "pe" and not dma:
                deps.discard(d)
            else:
                dop["inc"] = True
        return idx

    def build(self, nc):
        for b in self.bufs.values():
            if b.space == "dram":
                b.h = nc.dram_tensor(b.name, list(b.shape), b.dtype,
                                     kind=b.kind).ap()
        def lkey(op):
            s, k = op["lane"]
            return (s, k) if k == "c" else (s, k, op["ord"] % 8)

        def okey(op):
            return op["ord"] if op["lane"][1] == "c" else op["ord"] // 8

        import contextlib
        stack = contextlib.ExitStack()
        sems = {}
        with stack:
            for b in self.bufs.values():
                if b.space == "sbuf" and b.arena is None:
                    b.h = stack.enter_context(
                        nc.sbuf_tensor(b.name, list(b.shape), b.dtype))
                elif b.space == "psum":
                    b.h = stack.enter_context(
                        nc.psum_tensor(b.name, list(b.shape), b.dtype))
            lkeys = set()
            for lane, idxs in self.lanes.items():
                if lane[1] == "d":
                    for i in idxs:
                        self.ops[i]["inc"] = True
                        lkeys.add(lkey(self.ops[i]))
                if idxs:
                    self.ops[idxs[-1]]["inc"] = True
                    lkeys.add(lkey(self.ops[idxs[-1]]))
            for k in lkeys:
                sems[k] = stack.enter_context(
                    nc.semaphore("sem_" + "_".join(str(x) for x in k)))

            cnt = {}
            for lane, idxs in self.lanes.items():
                if lane[1] != "c":
                    continue
                c, pref = 0, []
                for i in idxs:
                    if self.ops[i]["inc"]:
                        c += 1
                    pref.append(c)
                cnt[lane] = pref

            def wait_val(dop):
                if dop["lane"][1] == "d":
                    return (okey(dop) + 1) * 16
                return cnt[dop["lane"]][dop["ord"]]

            vclock = [None] * len(self.ops)
            for op in self.ops:
                vc = {}
                for d in op["deps"]:
                    dop = self.ops[d]
                    for k, v in vclock[d].items():
                        if vc.get(k, -1) < v:
                            vc[k] = v
                    dk = lkey(dop)
                    if vc.get(dk, -1) < okey(dop):
                        vc[dk] = okey(dop)
                vclock[op["idx"]] = vc

            streams = {}
            for lane, idxs in self.lanes.items():
                streams.setdefault(lane[0], []).extend(idxs)
            block = stack.enter_context(nc.Block())
            for stream, idxs in streams.items():
                idxs.sort()
                section = getattr(block, STREAM_SECTION[stream])

                def run(eng, idxs=idxs):
                    observed = {}
                    for i in idxs:
                        op = self.ops[i]
                        for d in sorted(op["deps"]):
                            dop = self.ops[d]
                            dk = lkey(dop)
                            if dk == lkey(op) and dk == ("pe", "c"):
                                continue
                            if observed.get(dk, -1) >= okey(dop):
                                continue
                            eng.wait_ge(sems[dk], wait_val(dop))
                            for k, v in vclock[d].items():
                                if observed.get(k, -1) < v:
                                    observed[k] = v
                            if observed.get(dk, -1) < okey(dop):
                                observed[dk] = okey(dop)
                        inst = op["fn"](eng)
                        if op["inc"]:
                            k = lkey(op)
                            inst.then_inc(sems[k],
                                          16 if op["lane"][1] == "d" else 1)

                section(run)

            @block.sync
            def _(eng):
                for lane, idxs in self.lanes.items():
                    if not idxs:
                        continue
                    if lane[1] == "c":
                        c = cnt[lane][-1]
                        if c:
                            eng.wait_ge(sems[lane], c)
                    else:
                        n = len(idxs)
                        for sl in range(min(8, n)):
                            last = (n - 1 - sl) // 8 + 1
                            k = (lane[0], lane[1], (n - 1 - sl) % 8)
                            eng.wait_ge(sems[k], last * 16)
        return nc


def dma(P, q, dst, dfn, src, sfn, tag=""):
    P.emit(q, lambda e: e.dma_start(out=dfn(), in_=sfn()),
           reads=[src] if isinstance(src, (tuple, Buf)) else list(src),
           writes=[dst] if isinstance(dst, (tuple, Buf)) else list(dst),
           dma=True, tag=tag)



def build_launch1(debug=False):
    P = Prog("l1")

    x0s = P.dram("x0s", [D, TW], F32, "ExternalInput")
    gfm = P.dram("gfm", [D, G], F32, "ExternalInput")
    w = {}
    for nm in ["l0_wqT", "l0_wkT", "l0_wvT", "l0_woT",
               "l1_wqT", "l1_wkT", "l1_wvT"]:
        w[nm] = P.dram(nm, [D, D], F32, "ExternalInput")
    w["l0_upT"] = P.dram("l0_upT", [D, D_HID], F32, "ExternalInput")
    w["l0_downT"] = P.dram("l0_downT", [D_HID, D], F32, "ExternalInput")
    ln_s = P.dram("ln_s", [128, FC], F32, "ExternalInput")
    ln_b = P.dram("ln_b", [128, FC], F32, "ExternalInput")
    ups = P.dram("ups", [128, 64], F32, "ExternalInput")
    dns = P.dram("dns", [128, FC], F32, "ExternalInput")
    oscal = P.dram("oscal", [1, 1], F32, "ExternalInput")
    bq = P.dram("bq16", [128, FC], F32, "ExternalInput")
    bk = P.dram("bk16", [128, FC], F32, "ExternalInput")
    bv = P.dram("bv16", [128, FC], F32, "ExternalInput")
    m0 = P.dram("m0", [2, 6, 128, 512], F32, "ExternalInput")
    idf = P.dram("identf", [128, 128], F32, "ExternalInput")

    x2_o = P.dram("x2_o", [D, T_OWN], F32, "ExternalOutput")
    q1_o = P.dram("q1_o", [D, T_OWN], BF16, "ExternalOutput")
    k1_o = P.dram("k1_o", [D, T_OWN], BF16, "ExternalOutput")
    v1_o = P.dram("v1_o", [T_OWN, D], BF16, "ExternalOutput")
    qn1_o = P.dram("qn1_o", [1, N_HEADS * T_OWN], F32, "ExternalOutput")
    kn1_o = P.dram("kn1_o", [1, N_HEADS], F32, "ExternalOutput")

    kd = "ExternalOutput" if debug else "Internal"
    xn_d = P.dram("xn_d", [D, TK], BF16, kd)
    q0_d = P.dram("q0_d", [D, T_OWN], BF16, kd)
    k0_d = P.dram("k0_d", [D, TK], BF16, kd)
    v0_d = P.dram("v0_d", [TK, D], BF16, kd)
    r01_d = P.dram("r01_d", [D, T_OWN], F32, "Internal")
    attn_d = P.dram("attn_d", [D, T_OWN], BF16, kd)
    r1_d = P.dram("r1_d", [D, T_OWN], F32, kd)
    xn2_d = P.dram("xn2_d", [D, T_OWN], BF16, kd)

    AR = P.sbuf("AR", 128, 72 * 1024, BF16)

    ev = P.sbuf("ev", 128, 512, BF16)
    ev2 = P.sbuf("ev2", 128, 512, F32)
    bc2 = P.sbuf("bc2", 128, 512, F32)
    rows = P.sbuf("rows", 1, 512 * 8, F32)
    rowbf = P.sbuf("rowbf", 1, 512, BF16)
    ones_f = P.sbuf("ones_f", 128, 1, F32)
    ones_b = P.sbuf("ones_b", 128, 1, BF16)
    onesr_f = P.sbuf("onesr_f", 1, 128, F32)
    onesr_b = P.sbuf("onesr_b", 1, 128, BF16)
    lnsv = P.sbuf("lnsv", 128, FC, F32)
    lnbv = P.sbuf("lnbv", 128, FC, F32)
    upsv = P.sbuf("upsv", 128, 64, F32)
    dnsv = P.sbuf("dnsv", 128, FC, F32)
    dnscmb = P.sbuf("dnscmb", 128, FC, F32)
    osc = P.sbuf("osc", 1, 1, F32)
    bqv = P.sbuf("bqv", 128, FC, F32)
    bkv = P.sbuf("bkv", 128, FC, F32)
    bvv = P.sbuf("bvv", 128, FC, F32)
    bvrow = P.sbuf("bvrow", 1, D, F32)
    idf_sb = P.sbuf("idf_sb", 128, 128, F32)
    qh = P.sbuf("qh", 128, 512, BF16)
    kwin = P.sbuf("kwin", 128, 768 + G, BF16)
    vwin = P.sbuf("vwin", 128, 7 * 128, BF16)
    pt = P.sbuf("pt", 128, 7 * 512, BF16)
    rb = P.sbuf("rb", 128, 512, F32)
    knrow = P.sbuf("knrow", 1, 64, F32)
    eps1 = P.sbuf("eps1", 1, 1, F32)
    kn1r = P.sbuf("kn1r", 1, N_HEADS, F32)

    KB = 1024
    x0b = P.view("x0b", 128, FC * 512, F32, AR, 0)
    sqv = P.view("sqv", 128, 8 * 512, F32, AR, 64 * KB)
    ln0b = P.view("ln0b", 128, FC * 512, BF16, AR, 80 * KB)
    xnb = P.view("xnb", 128, FC * 512, BF16, AR, 96 * KB)
    ws0 = P.view("ws0", 128, FC * 512, BF16, AR, 0)
    ws1b = P.view("ws1b", 128, FC * 512, BF16, AR, 16 * KB)
    m0b = P.view("m0b", 128, 2 * 6 * 512, BF16, AR, 32 * KB)
    ablk = P.view("ablk", 128, FC * 512, BF16, AR, 44 * KB)
    rblk = P.view("rblk", 128, FC * 512, F32, AR, 112 * KB)
    xn1 = P.view("xn1", 128, FC * 256, BF16, AR, 32 * KB)
    hbuf = P.view("hbuf", 128, 64 * 256, BF16, AR, 44 * KB)
    sq5 = P.view("sq5", 128, 8 * 256, F32, AR, 76 * KB)
    xn2b = P.view("xn2b", 128, FC * 256, BF16, AR, 84 * KB)

    ps = [P.psum(f"ps{i}", 128, 512) for i in range(6)]
    ps_a = P.psum("ps_a", 128, 512)
    ps_b = P.psum("ps_b", 128, 512)

    P.emit("dve", lambda e: e.memset(eps1.h[:], LN_EPS), writes=[eps1])
    P.emit("dve", lambda e: e.memset(ones_f.h[:], 1.0), writes=[ones_f])
    P.emit("dve", lambda e: e.memset(ones_b.h[:], 1.0), writes=[ones_b])
    P.emit("dve", lambda e: e.memset(onesr_f.h[:], 1.0), writes=[onesr_f])
    P.emit("dve", lambda e: e.memset(onesr_b.h[:], 1.0), writes=[onesr_b])
    for (buf, src) in [(lnsv, ln_s), (lnbv, ln_b), (upsv, ups), (dnsv, dns),
                       (osc, oscal), (bqv, bq), (bkv, bk), (bvv, bv),
                       (idf_sb, idf)]:
        dma(P, "sp", (buf, 0, 1 << 58), lambda b=buf: b.h[:],
            (src, 0, 1 << 58), lambda s=src: s.h[:])
    P.emit("pe", lambda e: e.matmul(ps_a.h[:, 0:1], onesr_f.h[:], osc.h[:],
                                    start=True, stop=True),
           reads=[onesr_f, osc], writes=[(ps_a, 0, 1)])
    P.emit("dve", lambda e: e.tensor_copy(ev2.h[:, 0:1], ps_a.h[:, 0:1]),
           reads=[(ps_a, 0, 1)], writes=[(ev2, 0, 1)])
    P.emit("dve", lambda e: e.tensor_scalar(dnscmb.h[:], dnsv.h[:],
                                            ev2.h[:, 0:1], None, ALU.mult),
           reads=[dnsv, (ev2, 0, 1)], writes=[dnscmb])
    for fc in range(FC):
        P.emit("pe", lambda e, fc=fc: e.transpose(
            ps_a.h[0:1, 0:128], bvv.h[:, fc:fc + 1], idf_sb.h[:]),
            reads=[bvv, idf_sb], writes=[(ps_a, 0, 128)])
        P.emit("dve", lambda e, fc=fc: e.tensor_copy(
            bvrow.h[0:1, fc * 128:(fc + 1) * 128], ps_a.h[0:1, 0:128]),
            reads=[(ps_a, 0, 128)], writes=[(bvrow, fc * 128, (fc + 1) * 128)])

    def ln_block(src, TB, SRCW, sq, SQW, dst, affine, tag):
        nT = TB
        for half in range(2):
            f0 = half * 8
            P.emit("act", lambda e, f0=f0: e.activation(
                sq.h.ap().rearrange("p (fc t) -> p fc t", t=SQW)[:, :, 0:nT],
                src.h.ap().rearrange("p (fc t) -> p fc t", t=SRCW)
                [:, f0:f0 + 8, 0:nT], AF.Square),
                reads=[(src, f0 * SRCW, (f0 + 8) * SRCW)],
                writes=[(sq, 0, 8 * SQW)], tag=tag + "sq")
            for fc in range(8):
                fcg = f0 + fc
                P.emit("pe", lambda e, fc=fc, fcg=fcg: e.matmul(
                    ps_b.h[0:1, 0:nT], ones_f.h[:],
                    sq.h[:, fc * SQW: fc * SQW + nT],
                    start=(fcg == 0), stop=(fcg == FC - 1)),
                    reads=[ones_f, (sq, fc * SQW, fc * SQW + nT)],
                    writes=[(ps_b, 0, nT)], tag=tag + "ssq")
        for fc in range(FC):
            P.emit("pe", lambda e, fc=fc: e.matmul(
                ps_a.h[0:1, 0:nT], ones_f.h[:],
                src.h[:, fc * SRCW: fc * SRCW + nT],
                start=(fc == 0), stop=(fc == FC - 1)),
                reads=[ones_f, (src, fc * SRCW, fc * SRCW + nT)],
                writes=[(ps_a, 0, nT)], tag=tag + "ssum")
        MU, RST, NMU = 0, 512, 1024
        P.emit("dve", lambda e: e.tensor_single_scalar(
            rows.h[0:1, MU:MU + nT], ps_a.h[0:1, 0:nT], 1.0 / D, ALU.mult),
            reads=[(ps_a, 0, nT)], writes=[(rows, MU, MU + nT)])
        P.emit("dve", lambda e: e.tensor_single_scalar(
            rows.h[0:1, RST:RST + nT], ps_b.h[0:1, 0:nT], 1.0 / D, ALU.mult),
            reads=[(ps_b, 0, nT)], writes=[(rows, RST, RST + nT)])
        P.emit("dve", lambda e: e.tensor_tensor(
            rows.h[0:1, NMU:NMU + nT], rows.h[0:1, MU:MU + nT],
            rows.h[0:1, MU:MU + nT], ALU.mult),
            reads=[(rows, MU, MU + nT)], writes=[(rows, NMU, NMU + nT)])
        P.emit("dve", lambda e: e.tensor_tensor(
            rows.h[0:1, RST:RST + nT], rows.h[0:1, RST:RST + nT],
            rows.h[0:1, NMU:NMU + nT], ALU.subtract),
            reads=[(rows, RST, RST + nT), (rows, NMU, NMU + nT)],
            writes=[(rows, RST, RST + nT)])
        P.emit("act", lambda e: e.activation(
            rows.h[0:1, RST:RST + nT], rows.h[0:1, RST:RST + nT], AF.Sqrt,
            bias=eps1.h[:]), reads=[(rows, RST, RST + nT), eps1],
            writes=[(rows, RST, RST + nT)])
        P.emit("dve", lambda e: e.reciprocal(
            rows.h[0:1, RST:RST + nT], rows.h[0:1, RST:RST + nT]),
            reads=[(rows, RST, RST + nT)], writes=[(rows, RST, RST + nT)])
        P.emit("dve", lambda e: e.scalar_tensor_tensor(
            rows.h[0:1, NMU:NMU + nT], rows.h[0:1, MU:MU + nT], -1.0,
            rows.h[0:1, RST:RST + nT], ALU.mult, ALU.mult),
            reads=[(rows, MU, MU + nT), (rows, RST, RST + nT)],
            writes=[(rows, NMU, NMU + nT)])
        P.emit("pe", lambda e: e.matmul(
            ps_a.h[:, 0:nT], onesr_f.h[:], rows.h[0:1, RST:RST + nT],
            start=True, stop=True),
            reads=[onesr_f, (rows, RST, RST + nT)], writes=[(ps_a, 0, nT)])
        P.emit("pe", lambda e: e.matmul(
            ps_b.h[:, 0:nT], onesr_f.h[:], rows.h[0:1, NMU:NMU + nT],
            start=True, stop=True),
            reads=[onesr_f, (rows, NMU, NMU + nT)], writes=[(ps_b, 0, nT)])
        P.emit("dve", lambda e: e.tensor_copy(ev2.h[:, 0:nT], ps_a.h[:, 0:nT]),
               reads=[(ps_a, 0, nT)], writes=[(ev2, 0, nT)])
        P.emit("dve", lambda e: e.tensor_copy(bc2.h[:, 0:nT], ps_b.h[:, 0:nT]),
               reads=[(ps_b, 0, nT)], writes=[(bc2, 0, nT)])
        for fc in range(FC):
            dtgt, dlo = dst(fc)
            P.emit("dve", lambda e, fc=fc: e.tensor_tensor(
                sq.h[:, 0:nT], src.h[:, fc * SRCW:fc * SRCW + nT],
                ev2.h[:, 0:nT], ALU.mult),
                reads=[(src, fc * SRCW, fc * SRCW + nT), (ev2, 0, nT)],
                writes=[(sq, 0, nT)], tag=tag + "ap1")
            if affine is None:
                P.emit("dve", lambda e, dtgt=dtgt, dlo=dlo: e.tensor_tensor(
                    dtgt.h[:, dlo:dlo + nT], sq.h[:, 0:nT],
                    bc2.h[:, 0:nT], ALU.add),
                    reads=[(sq, 0, nT), (bc2, 0, nT)],
                    writes=[(dtgt, dlo, dlo + nT)], tag=tag + "ap2")
            else:
                asc, abi = affine
                P.emit("dve", lambda e: e.tensor_tensor(
                    sq.h[:, SQW:SQW + nT], sq.h[:, 0:nT],
                    bc2.h[:, 0:nT], ALU.add),
                    reads=[(sq, 0, nT), (bc2, 0, nT)],
                    writes=[(sq, SQW, SQW + nT)], tag=tag + "ap2")
                P.emit("act", lambda e, fc=fc, dtgt=dtgt, dlo=dlo:
                       e.activation(dtgt.h[:, dlo:dlo + nT],
                                    sq.h[:, SQW:SQW + nT], AF.Identity,
                                    bias=abi.h[:, fc:fc + 1],
                                    scale=asc.h[:, fc:fc + 1]),
                       reads=[(sq, SQW, SQW + nT), asc, abi],
                       writes=[(dtgt, dlo, dlo + nT)], tag=tag + "ap3")

    for (b0, b1) in [(0, 512), (512, 1024), (1024, 1280)]:
        TB = b1 - b0
        dma(P, "sp", (x0b, 0, FC * 512), lambda TB=TB:
            x0b.h.ap().rearrange("p (fc t) -> p fc t", t=512)[:, :, 0:TB],
            (x0s, 0, 1 << 58), lambda b0=b0, b1=b1:
            x0s.h.rearrange("(fc p) t -> p fc t", p=128)[:, :, b0:b1])
        ln_block(x0b, TB, 512, sqv, 512, lambda fc: (ln0b, fc * 512), None,
                 f"ln0_{b0}_")
        o0 = max(b0, HALO)
        if o0 < b1:
            for half in range(2):
                f0 = half * 8
                for fc in range(f0, f0 + 8):
                    P.emit("dve", lambda e, fc=fc, f0=f0, o0=o0, b1=b1, b0=b0:
                           e.tensor_tensor(
                               sqv.h[:, (fc - f0) * 512:
                                     (fc - f0) * 512 + b1 - o0],
                               x0b.h[:, fc * 512 + o0 - b0: fc * 512 + b1 - b0],
                               ln0b.h[:, fc * 512 + o0 - b0: fc * 512 + b1 - b0],
                               ALU.add),
                           reads=[(x0b, fc * 512 + o0 - b0, fc * 512 + b1 - b0),
                                  (ln0b, fc * 512 + o0 - b0, fc * 512 + b1 - b0)],
                           writes=[(sqv, (fc - f0) * 512,
                                    (fc - f0) * 512 + b1 - o0)],
                           tag="r01")
                dma(P, "sp", (r01_d, 0, 1 << 58),
                    lambda f0=f0, o0=o0, b1=b1:
                    r01_d.h.rearrange("(fc p) t -> p fc t", p=128)
                    [:, f0:f0 + 8, o0 - HALO:b1 - HALO],
                    (sqv, 0, 8 * 512), lambda o0=o0, b1=b1:
                    sqv.h.ap().rearrange("p (fc t) -> p fc t", t=512)
                    [:, :, 0:b1 - o0])
        P.emit("act", lambda e, TB=TB: e.activation(
            x0b.h.ap().rearrange("p (fc t) -> p fc t", t=512)[:, :, 0:TB],
            ln0b.h.ap().rearrange("p (fc t) -> p fc t", t=512)[:, :, 0:TB],
            AF.Identity), reads=[(ln0b, 0, FC * 512)],
            writes=[(x0b, 0, FC * 512)], tag="cp32")
        ln_block(x0b, TB, 512, sqv, 512, lambda fc: (xnb, fc * 512),
                 (lnsv, lnbv), f"lni_{b0}_")
        dma(P, "sp", (xn_d, 0, 1 << 58), lambda b0=b0, b1=b1:
            xn_d.h.rearrange("(fc p) t -> p fc t", p=128)[:, :, b0:b1],
            (xnb, 0, FC * 512), lambda TB=TB:
            xnb.h.ap().rearrange("p (fc t) -> p fc t", t=512)[:, :, 0:TB])
    dma(P, "pool", (xn_d, 0, 1 << 58), lambda:
        xn_d.h.rearrange("(fc p) t -> p fc t", p=128)[:, :, TW:TK],
        (gfm, 0, 1 << 58), lambda:
        gfm.h.rearrange("(fc p) t -> p fc t", p=128))

    def load_wslab(wname, o0, ow, sb, tag=""):
        dma(P, "pool", (sb, 0, 1 << 58), lambda sb=sb, ow=ow:
            sb.h.ap().rearrange("p (ic o) -> p ic o", o=512)[:, :, 0:ow],
            (w[wname], 0, 1 << 58), lambda wname=wname, o0=o0, ow=ow:
            w[wname].h.rearrange("(ic p) o -> p ic o", p=128)[:, :, o0:o0 + ow],
            tag=tag)

    def load_xnb(src_d, t0, TB, tag=""):
        dma(P, "sp", (xnb, 0, FC * 512), lambda TB=TB:
            xnb.h.ap().rearrange("p (fc t) -> p fc t", t=512)[:, :, 0:TB],
            (src_d, 0, 1 << 58), lambda src_d=src_d, t0=t0, TB=TB:
            src_d.h.rearrange("(fc p) t -> p fc t", p=128)[:, :, t0:t0 + TB],
            tag=tag)

    def proj_fm(wname, src_d, t0, t1, out_cb, tag=""):
        DO = w[wname].shape[1]
        for ob in range(0, DO, 512):
            sb = ws0 if (ob // 512) % 2 == 0 else ws1b
            load_wslab(wname, ob, 512, sb, tag=tag + f"w{ob}")
            for tb in range(t0, t1, 512):
                TB = min(512, t1 - tb)
                load_xnb(src_d, tb, TB, tag=tag + "x")
                for oc in range(ob, ob + 512, 128):
                    ocl = oc - ob
                    pp = ps[(oc // 128) % 2]
                    for ic in range(FC):
                        P.emit("pe", lambda e, ic=ic, ocl=ocl, TB=TB, sb=sb, pp=pp:
                               e.matmul(pp.h[:, 0:TB],
                                        sb.h[:, ic * 512 + ocl: ic * 512 + ocl + 128],
                                        xnb.h[:, ic * 512: ic * 512 + TB],
                                        start=(ic == 0), stop=(ic == FC - 1)),
                               reads=[(sb, ic * 512 + ocl, ic * 512 + ocl + 128),
                                      (xnb, ic * 512, ic * 512 + TB)],
                               writes=[(pp, 0, TB)], tag=tag + "mm")
                    out_cb(oc, tb, TB, pp)

    def evac_to_dram(dst, bias, dt0=0):
        def cb(oc, tb, TB, pp):
            if bias is None:
                P.emit("act", lambda e, TB=TB, pp=pp: e.activation(
                    ev.h[:, 0:TB], pp.h[:, 0:TB], AF.Identity),
                    reads=[(pp, 0, TB)], writes=[(ev, 0, TB)])
            else:
                P.emit("act", lambda e, oc=oc, TB=TB, pp=pp: e.activation(
                    ev.h[:, 0:TB], pp.h[:, 0:TB], AF.Identity,
                    bias=bias.h[:, oc // 128:oc // 128 + 1]),
                    reads=[(pp, 0, TB), bias], writes=[(ev, 0, TB)])
            dma(P, "sp", (dst, 0, 1 << 58),
                lambda dst=dst, oc=oc, tb=tb, TB=TB:
                dst.h[oc:oc + 128, tb - dt0:tb - dt0 + TB],
                (ev, 0, TB), lambda TB=TB: ev.h[:, 0:TB])
        return cb

    proj_fm("l0_wqT", xn_d, HALO, HALO + T_OWN,
            evac_to_dram(q0_d, None, dt0=HALO), tag="q0")
    proj_fm("l0_wkT", xn_d, 0, TK, evac_to_dram(k0_d, None), tag="k0")

    def proj_tm(wname, src_d, t0, t1, dst, brow, tag=""):
        DO = w[wname].shape[1]
        for ob in range(0, DO, 512):
            sb = ws0 if (ob // 512) % 2 == 0 else ws1b
            load_wslab(wname, ob, 512, sb, tag=tag + f"w{ob}")
            for tb in range(t0, t1, 512):
                TB = min(512, t1 - tb)
                load_xnb(src_d, tb, TB, tag=tag + "x")
                for tt in range(0, TB, 128):
                    tw = min(128, TB - tt)
                    pp = ps[(tt // 128) % 2]
                    for ic in range(FC):
                        P.emit("pe", lambda e, ic=ic, tt=tt, tw=tw, sb=sb, pp=pp:
                               e.matmul(pp.h[0:tw, 0:512],
                                        xnb.h[:, ic * 512 + tt: ic * 512 + tt + tw],
                                        sb.h[:, ic * 512:(ic + 1) * 512],
                                        start=(ic == 0), stop=(ic == FC - 1)),
                               reads=[(sb, ic * 512, (ic + 1) * 512),
                                      (xnb, ic * 512 + tt, ic * 512 + tt + tw)],
                               writes=[(pp, 0, 512)], tag=tag + "mm")
                    if brow is not None:
                        P.emit("pe", lambda e, tw=tw, pp=pp, ob=ob:
                               e.matmul(pp.h[0:tw, 0:512],
                                        onesr_f.h[0:1, 0:tw],
                                        brow.h[0:1, ob:ob + 512],
                                        start=False, stop=True),
                               reads=[onesr_f, (brow, ob, ob + 512)],
                               writes=[(pp, 0, 512)], tag=tag + "bf")
                    P.emit("act", lambda e, tw=tw, pp=pp: e.activation(
                        ev.h[0:tw, 0:512], pp.h[0:tw, 0:512], AF.Identity),
                        reads=[(pp, 0, 512)], writes=[(ev, 0, 512)])
                    dma(P, "sp", (dst, 0, 1 << 58),
                        lambda dst=dst, tb=tb, tt=tt, tw=tw, ob=ob:
                        dst.h[tb + tt:tb + tt + tw, ob:ob + 512],
                        (ev, 0, 512), lambda tw=tw: ev.h[0:tw, 0:512])

    proj_tm("l0_wvT", xn_d, 0, TK, v0_d, None, tag="v0")

    QN, NB, RR = 1536, 2048, 2560
    dma(P, "pool", (m0b, 0, 1 << 58), lambda:
        m0b.h.ap().rearrange("p (v r t) -> p v r t", r=6, t=512),
        (m0, 0, 1 << 58), lambda: m0.h.rearrange("v r p t -> p v r t"))

    def l0_attn(h, qc):
        t0 = qc * 512
        dma(P, "sp", (qh, 0, 512), lambda: qh.h[:],
            (q0_d, 0, 1 << 58), lambda h=h, t0=t0:
            q0_d.h[h * 128:(h + 1) * 128, t0:t0 + 512])
        dma(P, "sp", (kwin, 0, 768), lambda: kwin.h[:, 0:768],
            (k0_d, 0, 1 << 58), lambda h=h, t0=t0:
            k0_d.h[h * 128:(h + 1) * 128, t0:t0 + 768])
        dma(P, "sp", (kwin, 768, 768 + G), lambda: kwin.h[:, 768:768 + G],
            (k0_d, 0, 1 << 58), lambda h=h:
            k0_d.h[h * 128:(h + 1) * 128, TW:TK])
        dma(P, "sp", (vwin, 0, 6 * 128), lambda:
            vwin.h.ap().rearrange("p (kc d) -> p kc d", d=128)[:, 0:6, :],
            (v0_d, 0, 1 << 58), lambda h=h, t0=t0:
            v0_d.h[t0:t0 + 768, h * 128:(h + 1) * 128]
            .rearrange("(kc p) d -> p kc d", p=128))
        dma(P, "sp", (vwin, 6 * 128, 6 * 128 + G), lambda:
            vwin.h[0:G, 6 * 128:7 * 128],
            (v0_d, 0, 1 << 58), lambda h=h:
            v0_d.h[TW:TK, h * 128:(h + 1) * 128])
        P.emit("act", lambda e: e.activation(sqv.h[:, 0:512], qh.h[:],
                                             AF.Square),
               reads=[(qh, 0, 512)], writes=[(sqv, 0, 512)])
        P.emit("pe", lambda e: e.matmul(ps_a.h[0:1, 0:512], ones_f.h[:],
                                        sqv.h[:, 0:512], start=True, stop=True),
               reads=[ones_f, (sqv, 0, 512)], writes=[(ps_a, 0, 512)])
        P.emit("act", lambda e: e.activation(rows.h[0:1, QN:QN + 512],
                                             ps_a.h[0:1, 0:512], AF.Sqrt),
               reads=[(ps_a, 0, 512)], writes=[(rows, QN, QN + 512)])
        P.emit("act", lambda e: e.activation(sqv.h[:, 512:512 + 784],
                                             kwin.h[:, 0:784], AF.Square),
               reads=[(kwin, 0, 784)], writes=[(sqv, 512, 512 + 784)])
        P.emit("pe", lambda e: e.matmul(ps_a.h[0:1, 0:512], ones_f.h[:],
                                        sqv.h[:, 512:1024],
                                        start=True, stop=True),
               reads=[ones_f, (sqv, 512, 1024)], writes=[(ps_a, 0, 512)])
        P.emit("pe", lambda e: e.matmul(ps_b.h[0:1, 0:272], ones_f.h[:],
                                        sqv.h[:, 1024:1296],
                                        start=True, stop=True),
               reads=[ones_f, (sqv, 1024, 1296)], writes=[(ps_b, 0, 272)])
        P.emit("dve", lambda e: e.reduce_max(rows.h[0:1, RR:RR + 1],
                                             ps_a.h[0:1, 0:512], axis=AX.X),
               reads=[(ps_a, 0, 512)], writes=[(rows, RR, RR + 1)])
        P.emit("dve", lambda e: e.reduce_max(rows.h[0:1, RR + 1:RR + 2],
                                             ps_b.h[0:1, 0:272], axis=AX.X),
               reads=[(ps_b, 0, 272)], writes=[(rows, RR + 1, RR + 2)])
        P.emit("dve", lambda e: e.tensor_tensor(
            rows.h[0:1, RR:RR + 1], rows.h[0:1, RR:RR + 1],
            rows.h[0:1, RR + 1:RR + 2], ALU.max),
            reads=[(rows, RR, RR + 2)], writes=[(rows, RR, RR + 1)])
        P.emit("act", lambda e: e.activation(rows.h[0:1, RR:RR + 1],
                                             rows.h[0:1, RR:RR + 1], AF.Sqrt),
               reads=[(rows, RR, RR + 1)], writes=[(rows, RR, RR + 1)])
        P.emit("dve", lambda e: e.tensor_scalar(
            rows.h[0:1, NB:NB + 512], rows.h[0:1, QN:QN + 512],
            rows.h[0:1, RR:RR + 1], -1.0, ALU.mult, ALU.mult),
            reads=[(rows, QN, QN + 512), (rows, RR, RR + 1)],
            writes=[(rows, NB, NB + 512)])
        P.emit("act", lambda e: e.activation(rowbf.h[0:1, 0:512],
                                             rows.h[0:1, NB:NB + 512],
                                             AF.Identity),
               reads=[(rows, NB, NB + 512)], writes=[(rowbf, 0, 512)])
        for kt in range(6):
            pp = ps[2 + (kt % 2)]
            P.emit("pe", lambda e, kt=kt, pp=pp: e.matmul(
                pp.h[:, 0:512], kwin.h[:, kt * 128:(kt + 1) * 128], qh.h[:],
                start=True, stop=False),
                reads=[(kwin, kt * 128, (kt + 1) * 128), (qh, 0, 512)],
                writes=[(pp, 0, 512)], tag="l0st")
            P.emit("pe", lambda e, pp=pp: e.matmul(
                pp.h[:, 0:512], onesr_b.h[:], rowbf.h[0:1, 0:512],
                start=False, stop=True),
                reads=[onesr_b, (rowbf, 0, 512)], writes=[(pp, 0, 512)])
            moff = (qc * 6 + kt) * 512
            P.emit("dve", lambda e, pp=pp, moff=moff: e.tensor_tensor(
                pp.h[:, 0:512], pp.h[:, 0:512], m0b.h[:, moff:moff + 512],
                ALU.add),
                reads=[(pp, 0, 512), (m0b, moff, moff + 512)],
                writes=[(pp, 0, 512)], tag="l0msk")
            P.emit("act", lambda e, kt=kt, pp=pp: e.activation(
                pt.h[:, kt * 512:(kt + 1) * 512], pp.h[:, 0:512], AF.Exp,
                scale=SCALE_ATT),
                reads=[(pp, 0, 512)],
                writes=[(pt, kt * 512, (kt + 1) * 512)], tag="l0exp")
        ppg = ps[4]
        P.emit("pe", lambda e, ppg=ppg: e.matmul(
            ppg.h[0:G, 0:512], kwin.h[:, 768:768 + G], qh.h[:],
            start=True, stop=False),
            reads=[(kwin, 768, 768 + G), (qh, 0, 512)],
            writes=[(ppg, 0, 512)])
        P.emit("pe", lambda e, ppg=ppg: e.matmul(
            ppg.h[0:G, 0:512], onesr_b.h[0:1, 0:G], rowbf.h[0:1, 0:512],
            start=False, stop=True),
            reads=[onesr_b, (rowbf, 0, 512)], writes=[(ppg, 0, 512)])
        P.emit("act", lambda e, ppg=ppg: e.activation(
            pt.h[0:G, 6 * 512:7 * 512], ppg.h[0:G, 0:512], AF.Exp,
            scale=SCALE_ATT),
            reads=[(ppg, 0, 512)], writes=[(pt, 6 * 512, 7 * 512)],
            tag="l0exp")
        for kt in range(7):
            krows = G if kt == 6 else 128
            rowsrc = (pt, kt * 512, (kt + 1) * 512)
            P.emit("pe", lambda e, kt=kt, krows=krows: e.matmul(
                ps_a.h[0:1, 0:512], ones_b.h[0:krows, :],
                pt.h[0:krows, kt * 512:(kt + 1) * 512],
                start=(kt == 0), stop=(kt == 6)),
                reads=[ones_b, rowsrc], writes=[(ps_a, 0, 512)], tag="l0sum")
            P.emit("pe", lambda e, kt=kt, krows=krows: e.matmul(
                ps[5].h[:, 0:512],
                vwin.h[0:krows, kt * 128:(kt + 1) * 128],
                pt.h[0:krows, kt * 512:(kt + 1) * 512],
                start=(kt == 0), stop=(kt == 6)),
                reads=[(vwin, kt * 128, (kt + 1) * 128), rowsrc],
                writes=[(ps[5], 0, 512)], tag="l0pv")
        P.emit("dve", lambda e: e.reciprocal(rows.h[0:1, RR + 2:RR + 514],
                                             ps_a.h[0:1, 0:512]),
               reads=[(ps_a, 0, 512)], writes=[(rows, RR + 2, RR + 514)])
        P.emit("pe", lambda e: e.matmul(ps_b.h[:, 0:512], onesr_f.h[:],
                                        rows.h[0:1, RR + 2:RR + 514],
                                        start=True, stop=True),
               reads=[onesr_f, (rows, RR + 2, RR + 514)],
               writes=[(ps_b, 0, 512)])
        P.emit("dve", lambda e: e.tensor_copy(rb.h[:], ps_b.h[:, 0:512]),
               reads=[(ps_b, 0, 512)], writes=[rb])
        P.emit("dve", lambda e: e.tensor_tensor(
            ev.h[:, 0:512], ps[5].h[:, 0:512], rb.h[:], ALU.mult),
            reads=[(ps[5], 0, 512), rb], writes=[(ev, 0, 512)], tag="l0nrm")
        dma(P, "sp", (attn_d, 0, 1 << 58), lambda h=h, t0=t0:
            attn_d.h[h * 128:(h + 1) * 128, t0:t0 + 512],
            (ev, 0, 512), lambda: ev.h[:, 0:512])

    for h in range(N_HEADS):
        for qc in range(2):
            l0_attn(h, qc)

    for tb in range(0, T_OWN, 512):
        dma(P, "sp", (rblk, 0, FC * 512), lambda:
            rblk.h.ap().rearrange("p (fc t) -> p fc t", t=512),
            (r01_d, 0, 1 << 58), lambda tb=tb:
            r01_d.h.rearrange("(fc p) t -> p fc t", p=128)[:, :, tb:tb + 512])
        for ob in range(0, D, 512):
            sb = ws0 if (ob // 512) % 2 == 0 else ws1b
            load_wslab("l0_woT", ob, 512, sb, tag=f"wo{ob}")
            dma(P, "sp", (ablk, 0, FC * 512), lambda:
                ablk.h.ap().rearrange("p (fc t) -> p fc t", t=512),
                (attn_d, 0, 1 << 58), lambda tb=tb:
                attn_d.h.rearrange("(fc p) t -> p fc t", p=128)
                [:, :, tb:tb + 512])
            for oc in range(ob, ob + 512, 128):
                ocl = oc - ob
                pp = ps[(oc // 128) % 2]
                for ic in range(FC):
                    P.emit("pe", lambda e, ic=ic, ocl=ocl, sb=sb, pp=pp:
                           e.matmul(pp.h[:, 0:512],
                                    sb.h[:, ic * 512 + ocl: ic * 512 + ocl + 128],
                                    ablk.h[:, ic * 512:(ic + 1) * 512],
                                    start=(ic == 0), stop=(ic == FC - 1)),
                           reads=[(sb, ic * 512 + ocl, ic * 512 + ocl + 128),
                                  (ablk, ic * 512, (ic + 1) * 512)],
                           writes=[(pp, 0, 512)], tag="womm")
                fc = oc // 128
                P.emit("dve", lambda e, fc=fc, pp=pp: e.tensor_tensor(
                    rblk.h[:, fc * 512:(fc + 1) * 512],
                    rblk.h[:, fc * 512:(fc + 1) * 512],
                    pp.h[:, 0:512], ALU.add),
                    reads=[(pp, 0, 512), (rblk, fc * 512, (fc + 1) * 512)],
                    writes=[(rblk, fc * 512, (fc + 1) * 512)], tag="woadd")
        dma(P, "sp", (r1_d, 0, 1 << 58), lambda tb=tb:
            r1_d.h.rearrange("(fc p) t -> p fc t", p=128)[:, :, tb:tb + 512],
            (rblk, 0, FC * 512), lambda:
            rblk.h.ap().rearrange("p (fc t) -> p fc t", t=512))

    for tb in range(0, T_OWN, 256):
        dma(P, "sp", (rblk, 0, FC * 256), lambda:
            rblk.h.ap()[:, 0:FC * 256].rearrange("p (fc t) -> p fc t", t=256),
            (r1_d, 0, 1 << 58), lambda tb=tb:
            r1_d.h.rearrange("(fc p) t -> p fc t", p=128)[:, :, tb:tb + 256])
        ln_block(rblk, 256, 256, sq5, 256, lambda fc: (xn1, fc * 256), None,
                 f"ln1_{tb}_")
        for ob in range(0, D_HID, 512):
            sb = ws0 if (ob // 512) % 2 == 0 else ws1b
            load_wslab("l0_upT", ob, 512, sb, tag=f"up{ob}")
            P.emit("act", lambda e, sb=sb: e.activation(
                sb.h[:], sb.h[:], AF.Sign),
                reads=[(sb, 0, FC * 512)], writes=[(sb, 0, FC * 512)],
                tag="upsg")
            for oc in range(ob, ob + 512, 128):
                ocl = oc - ob
                pp = ps[(oc // 128) % 2]
                for ic in range(FC):
                    P.emit("pe", lambda e, ic=ic, ocl=ocl, sb=sb, pp=pp:
                           e.matmul(pp.h[:, 0:256],
                                    sb.h[:, ic * 512 + ocl: ic * 512 + ocl + 128],
                                    xn1.h[:, ic * 256:(ic + 1) * 256],
                                    start=(ic == 0), stop=(ic == FC - 1)),
                           reads=[(sb, ic * 512 + ocl, ic * 512 + ocl + 128),
                                  (xn1, ic * 256, (ic + 1) * 256)],
                           writes=[(pp, 0, 256)], tag="upmm")
                hc = oc // 128
                P.emit("act", lambda e, hc=hc, pp=pp: e.activation(
                    hbuf.h[:, hc * 256:(hc + 1) * 256], pp.h[:, 0:256],
                    AF.Relu, scale=upsv.h[:, hc:hc + 1]),
                    reads=[(pp, 0, 256), upsv],
                    writes=[(hbuf, hc * 256, (hc + 1) * 256)], tag="uprelu")
        for oc in range(0, D, 128):
            sb = ws0 if (oc // 128) % 2 == 0 else ws1b
            dma(P, "pool", (sb, 0, 64 * 128), lambda sb=sb:
                sb.h.ap().rearrange("p (ic o) -> p ic o", o=128)[:, 0:64, :],
                (w["l0_downT"], 0, 1 << 58), lambda oc=oc:
                w["l0_downT"].h.rearrange("(ic p) o -> p ic o", p=128)
                [:, :, oc:oc + 128],
                tag=f"dn{oc}")
            P.emit("act", lambda e, sb=sb: e.activation(
                sb.h[:, 0:64 * 128], sb.h[:, 0:64 * 128], AF.Sign),
                reads=[(sb, 0, 64 * 128)], writes=[(sb, 0, 64 * 128)],
                tag="dnsg")
            pp = ps[2 + (oc // 128) % 2]
            for ic in range(64):
                P.emit("pe", lambda e, ic=ic, sb=sb, pp=pp: e.matmul(
                    pp.h[:, 0:256], sb.h[:, ic * 128:(ic + 1) * 128],
                    hbuf.h[:, ic * 256:(ic + 1) * 256],
                    start=(ic == 0), stop=(ic == 63)),
                    reads=[(sb, ic * 128, (ic + 1) * 128),
                           (hbuf, ic * 256, (ic + 1) * 256)],
                    writes=[(pp, 0, 256)], tag="dnmm")
            fc = oc // 128
            P.emit("act", lambda e, fc=fc, pp=pp: e.activation(
                ev2.h[:, 0:256], pp.h[:, 0:256], AF.Identity,
                scale=dnscmb.h[:, fc:fc + 1]),
                reads=[(pp, 0, 256), dnscmb], writes=[(ev2, 0, 256)],
                tag="dnsc")
            P.emit("dve", lambda e, fc=fc: e.tensor_tensor(
                rblk.h[:, fc * 256:(fc + 1) * 256],
                rblk.h[:, fc * 256:(fc + 1) * 256], ev2.h[:, 0:256], ALU.add),
                reads=[(ev2, 0, 256), (rblk, fc * 256, (fc + 1) * 256)],
                writes=[(rblk, fc * 256, (fc + 1) * 256)], tag="dnadd")
        dma(P, "sp", (x2_o, 0, 1 << 58), lambda tb=tb:
            x2_o.h.rearrange("(fc p) t -> p fc t", p=128)[:, :, tb:tb + 256],
            (rblk, 0, FC * 256), lambda:
            rblk.h.ap()[:, 0:FC * 256].rearrange("p (fc t) -> p fc t", t=256))
        ln_block(rblk, 256, 256, sq5, 256, lambda fc: (xn2b, fc * 256), None,
                 f"ln2_{tb}_")
        dma(P, "sp", (xn2_d, 0, 1 << 58), lambda tb=tb:
            xn2_d.h.rearrange("(fc p) t -> p fc t", p=128)[:, :, tb:tb + 256],
            (xn2b, 0, FC * 256), lambda:
            xn2b.h.ap().rearrange("p (fc t) -> p fc t", t=256))

    proj_fm("l1_wqT", xn2_d, 0, T_OWN, evac_to_dram(q1_o, bqv), tag="q1")
    proj_fm("l1_wkT", xn2_d, 0, T_OWN, evac_to_dram(k1_o, bkv), tag="k1")
    proj_tm("l1_wvT", xn2_d, 0, T_OWN, v1_o, bvrow, tag="v1")

    for h in range(N_HEADS):
        for qc in range(2):
            dma(P, "sp", (qh, 0, 512), lambda: qh.h[:],
                (q1_o, 0, 1 << 58), lambda h=h, qc=qc:
                q1_o.h[h * 128:(h + 1) * 128, qc * 512:(qc + 1) * 512])
            P.emit("act", lambda e: e.activation(sqv.h[:, 0:512], qh.h[:],
                                                 AF.Square),
                   reads=[(qh, 0, 512)], writes=[(sqv, 0, 512)])
            P.emit("pe", lambda e: e.matmul(ps_a.h[0:1, 0:512], ones_f.h[:],
                                            sqv.h[:, 0:512], start=True,
                                            stop=True),
                   reads=[ones_f, (sqv, 0, 512)], writes=[(ps_a, 0, 512)])
            P.emit("act", lambda e: e.activation(
                rows.h[0:1, 0:512], ps_a.h[0:1, 0:512], AF.Sqrt),
                reads=[(ps_a, 0, 512)], writes=[(rows, 0, 512)])
            off = h * 1024 + qc * 512
            dma(P, "sp", (qn1_o, 0, 1 << 58), lambda off=off:
                qn1_o.h[0:1, off:off + 512],
                (rows, 0, 512), lambda: rows.h[0:1, 0:512])
        for qc in range(2):
            dma(P, "sp", (qh, 0, 512), lambda: qh.h[:],
                (k1_o, 0, 1 << 58), lambda h=h, qc=qc:
                k1_o.h[h * 128:(h + 1) * 128, qc * 512:(qc + 1) * 512])
            P.emit("act", lambda e: e.activation(sqv.h[:, 0:512], qh.h[:],
                                                 AF.Square),
                   reads=[(qh, 0, 512)], writes=[(sqv, 0, 512)])
            P.emit("pe", lambda e: e.matmul(ps_a.h[0:1, 0:512], ones_f.h[:],
                                            sqv.h[:, 0:512], start=True,
                                            stop=True),
                   reads=[ones_f, (sqv, 0, 512)], writes=[(ps_a, 0, 512)])
            P.emit("dve", lambda e, h=h, qc=qc: e.reduce_max(
                knrow.h[0:1, h * 2 + qc:h * 2 + qc + 1],
                ps_a.h[0:1, 0:512], axis=AX.X),
                reads=[(ps_a, 0, 512)],
                writes=[(knrow, h * 2 + qc, h * 2 + qc + 1)])
        P.emit("dve", lambda e, h=h: e.tensor_tensor(
            knrow.h[0:1, h * 2:h * 2 + 1], knrow.h[0:1, h * 2:h * 2 + 1],
            knrow.h[0:1, h * 2 + 1:h * 2 + 2], ALU.max),
            reads=[(knrow, h * 2, h * 2 + 2)],
            writes=[(knrow, h * 2, h * 2 + 1)])
        P.emit("act", lambda e, h=h: e.activation(
            kn1r.h[0:1, h:h + 1], knrow.h[0:1, h * 2:h * 2 + 1], AF.Sqrt),
            reads=[(knrow, h * 2, h * 2 + 1)], writes=[(kn1r, h, h + 1)])
    dma(P, "sp", (kn1_o, 0, 1 << 58), lambda: kn1_o.h[:],
        (kn1r, 0, 1 << 58), lambda: kn1r.h[:])

    return P



def _fmT(a):
    return np.ascontiguousarray(np.asarray(a, np.float32).T)


def _vec16(v):
    return np.ascontiguousarray(np.asarray(v, np.float32).reshape(-1, 128).T)


def _l0_masks(core):
    own = core * T_OWN
    m = np.zeros((2, 6, 128, 512), np.float32)
    j = np.arange(512)[None, :]
    i = np.arange(128)[:, None]
    for qc in range(2):
        q = own + qc * 512 + j
        for r in range(6):
            k = own - HALO + qc * 512 + r * 128 + i
            ok = (k <= q) & (k >= q - (WINDOW - 1)) & (k >= 0)
            m[qc, r] = np.where(ok, np.float32(0.0), np.float32(NEG))
    return m


def _build_l1_inputs(inputs, core):
    x0 = np.asarray(inputs["L0_attn_x"][0], np.float32)
    x0f = _fmT(x0)
    own = core * T_OWN
    sl = np.zeros((D, TW), np.float32)
    lo = max(0, own - HALO)
    sl[:, HALO - (own - lo):] = x0f[:, lo:own + T_OWN]
    return {
        "x0s": sl,
        "gfm": _fmT(np.asarray(inputs["L0_attn_global_tokens"][0], np.float32)),
        "l0_wqT": _fmT(inputs["L0_attn_Wq"]),
        "l0_wkT": _fmT(inputs["L0_attn_Wk"]),
        "l0_wvT": _fmT(inputs["L0_attn_Wv"]),
        "l0_woT": _fmT(inputs["L0_attn_Wo"]),
        "l0_upT": _fmT(inputs["L0_ffn_up_weight"]),
        "l0_downT": _fmT(inputs["L0_ffn_down_weight"]),
        "ln_s": _vec16(inputs["L0_attn_ln_scale"]),
        "ln_b": _vec16(inputs["L0_attn_ln_bias"]),
        "ups": _vec16(inputs["L0_ffn_up_scale"]),
        "dns": _vec16(inputs["L0_ffn_down_scale"]),
        "oscal": np.asarray(inputs["L0_ffn_output_scale"],
                            np.float32).reshape(1, 1),
        "l1_wqT": _fmT(inputs["L1_attn_wq"]),
        "l1_wkT": _fmT(inputs["L1_attn_wk"]),
        "l1_wvT": _fmT(inputs["L1_attn_wv"]),
        "bq16": _vec16(inputs["L1_attn_bq"]),
        "bk16": _vec16(inputs["L1_attn_bk"]),
        "bv16": _vec16(inputs["L1_attn_bv"]),
        "m0": _l0_masks(core),
        "identf": np.eye(128, dtype=np.float32),
    }


def _run(nc, in_maps):
    res = run_bass_kernel_spmd(nc, in_maps, core_ids=list(range(N_CORES)))
    return res.results





def build_launch2(debug=False, stop=None, nheads=N_HEADS, nkt=64):
    P = Prog("l2")
    NKT = nkt

    x2_i = P.dram("x2_i", [D, T_OWN], F32, "ExternalInput")
    q1_i = P.dram("q1_i", [D, T_OWN], BF16, "ExternalInput")
    krot = P.dram("krot", [2, D, T_FULL], BF16, "ExternalInput")
    vrot = P.dram("vrot", [2, N_HEADS, 128, T_FULL], BF16, "ExternalInput")
    qn1_i = P.dram("qn1_i", [1, N_HEADS * T_OWN], F32, "ExternalInput")
    knall = P.dram("knall", [1, 128], F32, "ExternalInput")
    flagcol = P.dram("flagcol", [128, 128], F32, "ExternalInput")
    dmask = P.dram("dmask", [4, 128, 512], F32, "ExternalInput")
    bo = P.dram("bo16", [128, FC], F32, "ExternalInput")
    woT = P.dram("l1_woT", [D, D], F32, "ExternalInput")
    wrt = P.dram("w_router", [D, N_EXP], F32, "ExternalInput")
    w1 = P.dram("w1", [N_EXP, D, F_EXP], F32, "ExternalInput")
    w2 = P.dram("w2", [N_EXP, F_EXP, D], F32, "ExternalInput")
    w3 = P.dram("w3", [N_EXP, D, F_EXP], F32, "ExternalInput")
    ws1 = P.dram("ws1", [D, FS], F32, "ExternalInput")
    ws2 = P.dram("ws2", [FS, D], F32, "ExternalInput")
    ws3 = P.dram("ws3", [D, FS], F32, "ExternalInput")
    iota8 = P.dram("iota8", [128, N_EXP], F32, "ExternalInput")

    out_o = P.dram("out_o", [D, T_OWN], F32, "ExternalOutput")
    kd = "ExternalOutput" if debug else "Internal"
    r3_d = P.dram("r3_d", [D, T_OWN], F32, kd)
    gates_d = P.dram("gates_d", [N_EXP, T_OWN], F32, kd)
    at1_d = P.dram("at1_d", [D, T_OWN], BF16, kd) if debug else None
    xn3_dbg = P.dram("xn3_dbg", [D, T_OWN], BF16, kd) if debug else None

    AR = P.sbuf("AR", 128, 80 * 1024, BF16)
    KB = 1024

    q1f = P.view("q1f", 128, FC * T_OWN, BF16, AR, 0)
    kh = [P.view(f"kh{i}", 128, T_FULL, BF16, AR, (32 + 16 * i) * KB)
          for i in range(2)]
    vh = [P.view(f"vh{i}", 128, T_FULL, BF16, AR, (64 + 16 * i) * KB)
          for i in range(2)]
    attn = P.view("attn", 128, FC * T_OWN, BF16, AR, 96 * KB)
    dmk = P.view("dmk", 128, 4 * 512, F32, AR, 128 * KB)
    ptb = [P.view(f"ptb{i}", 128, 512, BF16, AR, (136 + i) * KB)
           for i in range(2)]
    rblk = P.view("rblk", 128, FC * 512, F32, AR, 0)
    wsa = P.view("wsa", 128, FC * 512, BF16, AR, 32 * KB)
    wsb = P.view("wsb", 128, FC * 512, BF16, AR, 48 * KB)
    xn3 = P.view("xn3", 128, FC * T_OWN, BF16, AR, 64 * KB)
    sqL = P.view("sqL", 128, 8 * 512, F32, AR, 128 * KB)
    macc = P.view("macc", 128, FC * T_OWN, F32, AR, 0)
    wea = P.view("wea", 128, FC * 512, BF16, AR, 96 * KB)
    web = P.view("web", 128, FC * 512, BF16, AR, 112 * KB)
    hg2 = P.view("hg2", 128, 11 * 1024, BF16, AR, 128 * KB)
    silu_t = P.view("silu_t", 128, 4 * 512, BF16, AR, 150 * KB)
    hgs = P.view("hgs", 128, 22 * 512, BF16, AR, 128 * KB)
    rfin = P.view("rfin", 128, FC * 512, F32, AR, 96 * KB)

    ev = P.sbuf("ev", 128, 512, BF16)
    ev2 = P.sbuf("ev2", 128, 512, F32)
    bc2 = P.sbuf("bc2", 128, 512, F32)
    gb = P.sbuf("gb", 128, 512, F32)
    rows = P.sbuf("rows", 1, 512 * 8, F32)
    rowbf = P.sbuf("rowbf", 1, 512, BF16)
    ones_f = P.sbuf("ones_f", 128, 1, F32)
    ones_b = P.sbuf("ones_b", 128, 1, BF16)
    onesr_f = P.sbuf("onesr_f", 1, 128, F32)
    onesr_b = P.sbuf("onesr_b", 1, 128, BF16)
    eps1 = P.sbuf("eps1", 1, 1, F32)
    bov = P.sbuf("bov", 128, FC, F32)
    fcl_sb = P.sbuf("fcl_sb", 128, 128, F32)
    kn_sb = P.sbuf("kn_sb", 1, 128, F32)
    knm = P.sbuf("knm", 1, N_HEADS, F32)
    wr_sb = P.sbuf("wr_sb", 128, FC * N_EXP, BF16)
    io8 = P.sbuf("io8", 128, N_EXP, F32)
    mif = P.sbuf("mif", 128, N_EXP, F32)
    s8 = P.sbuf("s8", 128, N_EXP, F32)
    mm8 = P.sbuf("mm8", 128, N_EXP, F32)
    mi8 = P.sbuf("mi8", 128, N_EXP, U32)
    msk0 = P.sbuf("msk0", 128, N_EXP, F32)
    msk1 = P.sbuf("msk1", 128, N_EXP, F32)
    g12 = P.sbuf("g12", 128, 4, F32)
    call = P.sbuf("call", 128, N_EXP, F32)
    idf_sb = P.sbuf("idf_sb", 128, 128, F32)
    idf = P.dram("identf", [128, 128], F32, "ExternalInput")

    ps = [P.psum(f"ps{i}", 128, 512) for i in range(6)]
    ps_a = P.psum("ps_a", 128, 512)
    ps_b = P.psum("ps_b", 128, 512)

    P.emit("dve", lambda e: e.memset(eps1.h[:], LN_EPS), writes=[eps1])
    P.emit("dve", lambda e: e.memset(ones_f.h[:], 1.0), writes=[ones_f])
    P.emit("dve", lambda e: e.memset(ones_b.h[:], 1.0), writes=[ones_b])
    P.emit("dve", lambda e: e.memset(onesr_f.h[:], 1.0), writes=[onesr_f])
    P.emit("dve", lambda e: e.memset(onesr_b.h[:], 1.0), writes=[onesr_b])
    for (buf, src) in [(bov, bo), (fcl_sb, flagcol), (kn_sb, knall),
                       (io8, iota8), (idf_sb, idf)]:
        dma(P, "sp", (buf, 0, 1 << 58), lambda b=buf: b.h[:],
            (src, 0, 1 << 58), lambda s=src: s.h[:])
    dma(P, "sp", (dmk, 0, 1 << 58), lambda:
        dmk.h.ap().rearrange("p (k t) -> p k t", t=512),
        (dmask, 0, 1 << 58), lambda: dmask.h.rearrange("k p t -> p k t"))
    dma(P, "pool", (wr_sb, 0, 1 << 58), lambda:
        wr_sb.h.ap().rearrange("p (ic o) -> p ic o", o=N_EXP),
        (wrt, 0, 1 << 58), lambda:
        wrt.h.rearrange("(ic p) o -> p ic o", p=128))
    dma(P, "sp", (q1f, 0, 1 << 58), lambda:
        q1f.h.ap().rearrange("p (fc t) -> p fc t", t=T_OWN),
        (q1_i, 0, 1 << 58), lambda:
        q1_i.h.rearrange("(fc p) t -> p fc t", p=128))
    for h in range(N_HEADS):
        P.emit("dve", lambda e, h=h: e.reduce_max(
            knm.h[0:1, h:h + 1], kn_sb.h[0:1, h * 8:(h + 1) * 8], axis=AX.X),
            reads=[(kn_sb, h * 8, (h + 1) * 8)], writes=[(knm, h, h + 1)])

    QN, NB, RR = 1536, 2048, 2560

    def l1_attn(h, qc, step):
        kb, vb = kh[step % 2], vh[step % 2]
        dma(P, "sp", (kb, 0, T_FULL), lambda kb=kb: kb.h[:],
            (krot, 0, 1 << 58), lambda h=h, qc=qc:
            krot.h[qc, h * 128:(h + 1) * 128, :])
        dma(P, "sp", (vb, 0, T_FULL), lambda vb=vb: vb.h[:],
            (vrot, 0, 1 << 58), lambda h=h, qc=qc: vrot.h[qc, h])
        dma(P, "act", (rows, QN, QN + 512), lambda:
            rows.h[0:1, QN:QN + 512],
            (qn1_i, 0, 1 << 58), lambda h=h, qc=qc:
            qn1_i.h[0:1, h * T_OWN + qc * 512: h * T_OWN + qc * 512 + 512])
        P.emit("dve", lambda e, h=h: e.tensor_scalar(
            rows.h[0:1, NB:NB + 512], rows.h[0:1, QN:QN + 512],
            knm.h[0:1, h:h + 1], -1.0, ALU.mult, ALU.mult),
            reads=[(rows, QN, QN + 512), (knm, h, h + 1)],
            writes=[(rows, NB, NB + 512)])
        P.emit("act", lambda e: e.activation(rowbf.h[0:1, 0:512],
                                             rows.h[0:1, NB:NB + 512],
                                             AF.Identity),
               reads=[(rows, NB, NB + 512)], writes=[(rowbf, 0, 512)])
        qoff = h * T_OWN + qc * 512
        for kt in range(NKT):
            pp = ps[kt % 2]
            P.emit("pe", lambda e, kt=kt, pp=pp, kb=kb, qoff=qoff: e.matmul(
                pp.h[:, 0:512], kb.h[:, kt * 128:(kt + 1) * 128],
                q1f.h[:, qoff:qoff + 512], start=True, stop=False),
                reads=[(kb, kt * 128, (kt + 1) * 128), (q1f, qoff, qoff + 512)],
                writes=[(pp, 0, 512)], tag="l1st")
            P.emit("pe", lambda e, pp=pp: e.matmul(
                pp.h[:, 0:512], onesr_b.h[:], rowbf.h[0:1, 0:512],
                start=False, stop=True),
                reads=[onesr_b, (rowbf, 0, 512)], writes=[(pp, 0, 512)])
            if kt >= 60:
                P.emit("dve", lambda e, pp=pp, kt=kt: e.tensor_tensor(
                    pp.h[:, 0:512], pp.h[:, 0:512],
                    dmk.h[:, (kt - 60) * 512:(kt - 59) * 512], ALU.add),
                    reads=[(pp, 0, 512),
                           (dmk, (kt - 60) * 512, (kt - 59) * 512)],
                    writes=[(pp, 0, 512)], tag="l1dm")
            pb = ptb[kt % 2]
            P.emit("act", lambda e, pp=pp, pb=pb: e.activation(
                pb.h[:], pp.h[:, 0:512], AF.Exp, scale=SCALE_ATT),
                reads=[(pp, 0, 512)], writes=[pb], tag="l1exp")
            fidx = qc * 64 + kt
            P.emit("dve", lambda e, pb=pb, fidx=fidx: e.tensor_scalar(
                pb.h[:], pb.h[:], fcl_sb.h[:, fidx:fidx + 1], None, ALU.mult),
                reads=[pb, (fcl_sb, fidx, fidx + 1)], writes=[pb], tag="l1fl")
            P.emit("pe", lambda e, kt=kt, pb=pb: e.matmul(
                ps[2].h[0:1, 0:512], ones_b.h[:], pb.h[:],
                start=(kt == 0), stop=(kt == NKT - 1)),
                reads=[ones_b, pb], writes=[(ps[2], 0, 512)], tag="l1sum")
            P.emit("pe", lambda e, kt=kt, pb=pb, vb=vb: e.matmul(
                ps[3].h[:, 0:512], vb.h[:, kt * 128:(kt + 1) * 128], pb.h[:],
                start=(kt == 0), stop=(kt == NKT - 1)),
                reads=[(vb, kt * 128, (kt + 1) * 128), pb],
                writes=[(ps[3], 0, 512)], tag="l1pv")
        P.emit("dve", lambda e: e.reciprocal(rows.h[0:1, RR:RR + 512],
                                             ps[2].h[0:1, 0:512]),
               reads=[(ps[2], 0, 512)], writes=[(rows, RR, RR + 512)])
        P.emit("pe", lambda e: e.matmul(ps[4].h[:, 0:512], onesr_f.h[:],
                                        rows.h[0:1, RR:RR + 512],
                                        start=True, stop=True),
               reads=[onesr_f, (rows, RR, RR + 512)],
               writes=[(ps[4], 0, 512)])
        P.emit("dve", lambda e: e.tensor_copy(gb.h[:], ps[4].h[:, 0:512]),
               reads=[(ps[4], 0, 512)], writes=[gb])
        P.emit("dve", lambda e, qoff=qoff: e.tensor_tensor(
            attn.h[:, qoff:qoff + 512], ps[3].h[:, 0:512], gb.h[:], ALU.mult),
            reads=[(ps[3], 0, 512), gb], writes=[(attn, qoff, qoff + 512)],
            tag="l1nrm")

    step = 0
    for h in range(nheads):
        for qc in range(2):
            l1_attn(h, qc, step)
            step += 1

    if debug:
        dma(P, "sp", (at1_d, 0, 1 << 58), lambda:
            at1_d.h.rearrange("(fc p) t -> p fc t", p=128),
            (attn, 0, 1 << 58), lambda:
            attn.h.ap().rearrange("p (fc t) -> p fc t", t=T_OWN))

    if stop == "attn":
        return P
    def ln_block2(src, TB, SRCW, sq, SQW, dst, tag):
        nT = TB
        for half in range(2):
            f0 = half * 8
            P.emit("act", lambda e, f0=f0: e.activation(
                sq.h.ap().rearrange("p (fc t) -> p fc t", t=SQW)[:, :, 0:nT],
                src.h.ap()[:, 0:FC * SRCW]
                .rearrange("p (fc t) -> p fc t", t=SRCW)[:, f0:f0 + 8, 0:nT],
                AF.Square),
                reads=[(src, f0 * SRCW, (f0 + 8) * SRCW)],
                writes=[(sq, 0, 8 * SQW)], tag=tag + "sq")
            for fc in range(8):
                fcg = f0 + fc
                P.emit("pe", lambda e, fc=fc, fcg=fcg: e.matmul(
                    ps_b.h[0:1, 0:nT], ones_f.h[:],
                    sq.h[:, fc * SQW: fc * SQW + nT],
                    start=(fcg == 0), stop=(fcg == FC - 1)),
                    reads=[ones_f, (sq, fc * SQW, fc * SQW + nT)],
                    writes=[(ps_b, 0, nT)], tag=tag + "ssq")
        for fc in range(FC):
            P.emit("pe", lambda e, fc=fc: e.matmul(
                ps_a.h[0:1, 0:nT], ones_f.h[:],
                src.h[:, fc * SRCW: fc * SRCW + nT],
                start=(fc == 0), stop=(fc == FC - 1)),
                reads=[ones_f, (src, fc * SRCW, fc * SRCW + nT)],
                writes=[(ps_a, 0, nT)], tag=tag + "ssum")
        MU, RST, NMU = 0, 512, 1024
        P.emit("dve", lambda e: e.tensor_single_scalar(
            rows.h[0:1, MU:MU + nT], ps_a.h[0:1, 0:nT], 1.0 / D, ALU.mult),
            reads=[(ps_a, 0, nT)], writes=[(rows, MU, MU + nT)])
        P.emit("dve", lambda e: e.tensor_single_scalar(
            rows.h[0:1, RST:RST + nT], ps_b.h[0:1, 0:nT], 1.0 / D, ALU.mult),
            reads=[(ps_b, 0, nT)], writes=[(rows, RST, RST + nT)])
        P.emit("dve", lambda e: e.tensor_tensor(
            rows.h[0:1, NMU:NMU + nT], rows.h[0:1, MU:MU + nT],
            rows.h[0:1, MU:MU + nT], ALU.mult),
            reads=[(rows, MU, MU + nT)], writes=[(rows, NMU, NMU + nT)])
        P.emit("dve", lambda e: e.tensor_tensor(
            rows.h[0:1, RST:RST + nT], rows.h[0:1, RST:RST + nT],
            rows.h[0:1, NMU:NMU + nT], ALU.subtract),
            reads=[(rows, RST, RST + nT), (rows, NMU, NMU + nT)],
            writes=[(rows, RST, RST + nT)])
        P.emit("act", lambda e: e.activation(
            rows.h[0:1, RST:RST + nT], rows.h[0:1, RST:RST + nT], AF.Sqrt,
            bias=eps1.h[:]), reads=[(rows, RST, RST + nT), eps1],
            writes=[(rows, RST, RST + nT)])
        P.emit("dve", lambda e: e.reciprocal(
            rows.h[0:1, RST:RST + nT], rows.h[0:1, RST:RST + nT]),
            reads=[(rows, RST, RST + nT)], writes=[(rows, RST, RST + nT)])
        P.emit("dve", lambda e: e.scalar_tensor_tensor(
            rows.h[0:1, NMU:NMU + nT], rows.h[0:1, MU:MU + nT], -1.0,
            rows.h[0:1, RST:RST + nT], ALU.mult, ALU.mult),
            reads=[(rows, MU, MU + nT), (rows, RST, RST + nT)],
            writes=[(rows, NMU, NMU + nT)])
        P.emit("pe", lambda e: e.matmul(
            ps_a.h[:, 0:nT], onesr_f.h[:], rows.h[0:1, RST:RST + nT],
            start=True, stop=True),
            reads=[onesr_f, (rows, RST, RST + nT)], writes=[(ps_a, 0, nT)])
        P.emit("pe", lambda e: e.matmul(
            ps_b.h[:, 0:nT], onesr_f.h[:], rows.h[0:1, NMU:NMU + nT],
            start=True, stop=True),
            reads=[onesr_f, (rows, NMU, NMU + nT)], writes=[(ps_b, 0, nT)])
        P.emit("dve", lambda e: e.tensor_copy(ev2.h[:, 0:nT], ps_a.h[:, 0:nT]),
               reads=[(ps_a, 0, nT)], writes=[(ev2, 0, nT)])
        P.emit("dve", lambda e: e.tensor_copy(bc2.h[:, 0:nT], ps_b.h[:, 0:nT]),
               reads=[(ps_b, 0, nT)], writes=[(bc2, 0, nT)])
        for fc in range(FC):
            dtgt, dlo = dst(fc)
            P.emit("dve", lambda e, fc=fc: e.tensor_tensor(
                sq.h[:, 0:nT], src.h[:, fc * SRCW:fc * SRCW + nT],
                ev2.h[:, 0:nT], ALU.mult),
                reads=[(src, fc * SRCW, fc * SRCW + nT), (ev2, 0, nT)],
                writes=[(sq, 0, nT)], tag=tag + "ap1")
            P.emit("dve", lambda e, dtgt=dtgt, dlo=dlo: e.tensor_tensor(
                dtgt.h[:, dlo:dlo + nT], sq.h[:, 0:nT],
                bc2.h[:, 0:nT], ALU.add),
                reads=[(sq, 0, nT), (bc2, 0, nT)],
                writes=[(dtgt, dlo, dlo + nT)], tag=tag + "ap2")

    for tb in range(0, T_OWN, 512):
        dma(P, "sp", (rblk, 0, FC * 512), lambda:
            rblk.h.ap().rearrange("p (fc t) -> p fc t", t=512),
            (x2_i, 0, 1 << 58), lambda tb=tb:
            x2_i.h.rearrange("(fc p) t -> p fc t", p=128)[:, :, tb:tb + 512])
        for ob in range(0, D, 512):
            sb = wsa if (ob // 512) % 2 == 0 else wsb
            dma(P, "pool", (sb, 0, 1 << 58), lambda sb=sb:
                sb.h.ap().rearrange("p (ic o) -> p ic o", o=512),
                (woT, 0, 1 << 58), lambda ob=ob:
                woT.h.rearrange("(ic p) o -> p ic o", p=128)[:, :, ob:ob + 512],
                tag=f"wo{ob}")
            for oc in range(ob, ob + 512, 128):
                ocl = oc - ob
                pp = ps[(oc // 128) % 2]
                for ic in range(FC):
                    P.emit("pe", lambda e, ic=ic, ocl=ocl, sb=sb, pp=pp, tb=tb:
                           e.matmul(pp.h[:, 0:512],
                                    sb.h[:, ic * 512 + ocl: ic * 512 + ocl + 128],
                                    attn.h[:, ic * T_OWN + tb: ic * T_OWN + tb + 512],
                                    start=(ic == 0), stop=(ic == FC - 1)),
                           reads=[(sb, ic * 512 + ocl, ic * 512 + ocl + 128),
                                  (attn, ic * T_OWN + tb, ic * T_OWN + tb + 512)],
                           writes=[(pp, 0, 512)], tag="womm")
                fc = oc // 128
                P.emit("act", lambda e, fc=fc, pp=pp: e.activation(
                    ev2.h[:, 0:512], pp.h[:, 0:512], AF.Identity,
                    bias=bov.h[:, fc:fc + 1]),
                    reads=[(pp, 0, 512), bov], writes=[(ev2, 0, 512)])
                P.emit("dve", lambda e, fc=fc: e.tensor_tensor(
                    rblk.h[:, fc * 512:(fc + 1) * 512],
                    rblk.h[:, fc * 512:(fc + 1) * 512],
                    ev2.h[:, 0:512], ALU.add),
                    reads=[(ev2, 0, 512), (rblk, fc * 512, (fc + 1) * 512)],
                    writes=[(rblk, fc * 512, (fc + 1) * 512)], tag="woadd")
        dma(P, "sp", (r3_d, 0, 1 << 58), lambda tb=tb:
            r3_d.h.rearrange("(fc p) t -> p fc t", p=128)[:, :, tb:tb + 512],
            (rblk, 0, FC * 512), lambda:
            rblk.h.ap().rearrange("p (fc t) -> p fc t", t=512))
        ln_block2(rblk, 512, 512, sqL, 512,
                  lambda fc, tb=tb: (xn3, fc * T_OWN + tb), f"ln3_{tb}_")

    if debug:
        dma(P, "sp", (xn3_dbg, 0, 1 << 58), lambda:
            xn3_dbg.h.rearrange("(fc p) t -> p fc t", p=128),
            (xn3, 0, 1 << 58), lambda:
            xn3.h.ap().rearrange("p (fc t) -> p fc t", t=T_OWN))

    if stop == "pb":
        return P
    for tt in range(0, T_OWN, 128):
        pp = ps[5]
        for ic in range(FC):
            P.emit("pe", lambda e, ic=ic, tt=tt, pp=pp: e.matmul(
                pp.h[:, 0:N_EXP],
                xn3.h[:, ic * T_OWN + tt: ic * T_OWN + tt + 128],
                wr_sb.h[:, ic * N_EXP:(ic + 1) * N_EXP],
                start=(ic == 0), stop=(ic == FC - 1)),
                reads=[(xn3, ic * T_OWN + tt, ic * T_OWN + tt + 128),
                       (wr_sb, ic * N_EXP, (ic + 1) * N_EXP)],
                writes=[(pp, 0, N_EXP)], tag="rtmm")
        P.emit("act", lambda e, pp=pp: e.activation(
            s8.h[:], pp.h[:, 0:N_EXP], AF.Sigmoid),
            reads=[(pp, 0, N_EXP)], writes=[s8], tag="rtsg")
        P.emit("dve", lambda e: e.max(mm8.h[:], s8.h[:]),
               reads=[s8], writes=[mm8])
        P.emit("dve", lambda e: e.max_index(mi8.h[:], mm8.h[:], s8.h[:]),
               reads=[mm8, s8], writes=[mi8])
        P.emit("dve", lambda e: e.tensor_tensor(
            g12.h[:, 0:1], mm8.h[:, 0:1], mm8.h[:, 1:2], ALU.add),
            reads=[mm8], writes=[(g12, 0, 1)])
        P.emit("dve", lambda e: e.reciprocal(g12.h[:, 1:2], g12.h[:, 0:1]),
               reads=[(g12, 0, 1)], writes=[(g12, 1, 2)])
        P.emit("dve", lambda e: e.tensor_tensor(
            g12.h[:, 2:3], mm8.h[:, 0:1], g12.h[:, 1:2], ALU.mult),
            reads=[mm8, (g12, 1, 2)], writes=[(g12, 2, 3)])
        P.emit("dve", lambda e: e.tensor_tensor(
            g12.h[:, 3:4], mm8.h[:, 1:2], g12.h[:, 1:2], ALU.mult),
            reads=[mm8, (g12, 1, 2)], writes=[(g12, 3, 4)])
        P.emit("dve", lambda e: e.tensor_copy(mif.h[:], mi8.h[:]),
               reads=[mi8], writes=[mif])
        P.emit("dve", lambda e: e.tensor_scalar(
            msk0.h[:], io8.h[:], mif.h[:, 0:1], None, ALU.is_equal),
            reads=[io8, mif], writes=[msk0])
        P.emit("dve", lambda e: e.tensor_scalar(
            msk1.h[:], io8.h[:], mif.h[:, 1:2], None, ALU.is_equal),
            reads=[io8, mif], writes=[msk1])
        P.emit("dve", lambda e: e.tensor_scalar(
            msk0.h[:], msk0.h[:], g12.h[:, 2:3], None, ALU.mult),
            reads=[msk0, (g12, 2, 3)], writes=[msk0])
        P.emit("dve", lambda e: e.tensor_scalar(
            msk1.h[:], msk1.h[:], g12.h[:, 3:4], None, ALU.mult),
            reads=[msk1, (g12, 3, 4)], writes=[msk1])
        P.emit("dve", lambda e: e.tensor_tensor(
            call.h[:], msk0.h[:], msk1.h[:], ALU.add),
            reads=[msk0, msk1], writes=[call])
        P.emit("pe", lambda e: e.transpose(
            ps_a.h[0:N_EXP, 0:128], call.h[:], idf_sb.h[:]),
            reads=[call, idf_sb], writes=[(ps_a, 0, 128)])
        P.emit("dve", lambda e: e.tensor_copy(ev2.h[0:N_EXP, 0:128],
                                              ps_a.h[0:N_EXP, 0:128]),
               reads=[(ps_a, 0, 128)], writes=[(ev2, 0, 128)])
        dma(P, "act", (gates_d, 0, 1 << 58), lambda tt=tt:
            gates_d.h[:, tt:tt + 128],
            (ev2, 0, 128), lambda: ev2.h[0:N_EXP, 0:128])

    if stop == "router":
        return P
    P.emit("dve", lambda e: e.memset(macc.h[:], 0.0), writes=[macc])

    FEc = F_EXP // 128
    for ex in range(N_EXP):
        for tb in range(0, T_OWN, 512):
            dma(P, "act", (rows, 0, 512), lambda: rows.h[0:1, 0:512],
                (gates_d, 0, 1 << 58), lambda ex=ex, tb=tb:
                gates_d.h[ex:ex + 1, tb:tb + 512])
            P.emit("pe", lambda e: e.matmul(
                ps[4].h[:, 0:512], onesr_f.h[:], rows.h[0:1, 0:512],
                start=True, stop=True),
                reads=[onesr_f, (rows, 0, 512)], writes=[(ps[4], 0, 512)])
            P.emit("dve", lambda e: e.tensor_copy(gb.h[:], ps[4].h[:, 0:512]),
                   reads=[(ps[4], 0, 512)], writes=[gb], tag="gbc")
            for fs in range(3):
                f0 = fs * 512
                fw = min(512, F_EXP - f0)
                nfc = fw // 128
                dma(P, "pool", (wea, 0, 1 << 58), lambda fw=fw:
                    wea.h.ap().rearrange("p (ic o) -> p ic o", o=512)
                    [:, :, 0:fw],
                    (w1, 0, 1 << 58), lambda ex=ex, f0=f0, fw=fw:
                    w1.h[ex].rearrange("(ic p) o -> p ic o", p=128)
                    [:, :, f0:f0 + fw], tag=f"w1_{ex}_{fs}")
                dma(P, "pool", (web, 0, 1 << 58), lambda fw=fw:
                    web.h.ap().rearrange("p (ic o) -> p ic o", o=512)
                    [:, :, 0:fw],
                    (w3, 0, 1 << 58), lambda ex=ex, f0=f0, fw=fw:
                    w3.h[ex].rearrange("(ic p) o -> p ic o", p=128)
                    [:, :, f0:f0 + fw], tag=f"w3_{ex}_{fs}")
                for fcl in range(nfc):
                    ocl = fcl * 128
                    pp = ps[fcl % 2]
                    for ic in range(FC):
                        P.emit("pe", lambda e, ic=ic, ocl=ocl, tb=tb, pp=pp:
                               e.matmul(pp.h[:, 0:512],
                                        wea.h[:, ic * 512 + ocl: ic * 512 + ocl + 128],
                                        xn3.h[:, ic * T_OWN + tb: ic * T_OWN + tb + 512],
                                        start=(ic == 0), stop=(ic == FC - 1)),
                               reads=[(wea, ic * 512 + ocl, ic * 512 + ocl + 128),
                                      (xn3, ic * T_OWN + tb, ic * T_OWN + tb + 512)],
                               writes=[(pp, 0, 512)], tag="h1mm")
                    P.emit("act", lambda e, fcl=fcl, pp=pp: e.activation(
                        silu_t.h[:, fcl * 512:(fcl + 1) * 512],
                        pp.h[:, 0:512], AF.Silu),
                        reads=[(pp, 0, 512)],
                        writes=[(silu_t, fcl * 512, (fcl + 1) * 512)],
                        tag="silu")
                for fcl in range(nfc):
                    ocl = fcl * 128
                    pp = ps[2 + fcl % 2]
                    for ic in range(FC):
                        P.emit("pe", lambda e, ic=ic, ocl=ocl, tb=tb, pp=pp:
                               e.matmul(pp.h[:, 0:512],
                                        web.h[:, ic * 512 + ocl: ic * 512 + ocl + 128],
                                        xn3.h[:, ic * T_OWN + tb: ic * T_OWN + tb + 512],
                                        start=(ic == 0), stop=(ic == FC - 1)),
                               reads=[(web, ic * 512 + ocl, ic * 512 + ocl + 128),
                                      (xn3, ic * T_OWN + tb, ic * T_OWN + tb + 512)],
                               writes=[(pp, 0, 512)], tag="h3mm")
                    fcg = fs * 4 + fcl
                    P.emit("dve", lambda e, fcl=fcl, pp=pp: e.tensor_tensor(
                        ev.h[:, 0:512], silu_t.h[:, fcl * 512:(fcl + 1) * 512],
                        pp.h[:, 0:512], ALU.mult),
                        reads=[(silu_t, fcl * 512, (fcl + 1) * 512),
                               (pp, 0, 512)],
                        writes=[(ev, 0, 512)], tag="hgm1")
                    hoff = fcg * T_OWN + tb
                    P.emit("dve", lambda e, hoff=hoff: e.tensor_tensor(
                        hg2.h[:, hoff:hoff + 512], ev.h[:, 0:512],
                        gb.h[:], ALU.mult),
                        reads=[(ev, 0, 512), gb],
                        writes=[(hg2, hoff, hoff + 512)], tag="hgm2")
        for oc in range(0, D, 128):
            sb = wea if (oc // 128) % 2 == 0 else web
            dma(P, "pool", (sb, 0, FEc * 128), lambda sb=sb:
                sb.h.ap()[:, 0:FEc * 128]
                .rearrange("p (ic o) -> p ic o", o=128),
                (w2, 0, 1 << 58), lambda ex=ex, oc=oc:
                w2.h[ex].rearrange("(ic p) o -> p ic o", p=128)
                [:, :, oc:oc + 128], tag=f"w2_{ex}_{oc}")
            for tb in range(0, T_OWN, 512):
                pp = ps[4 + (tb // 512) % 2]
                for ic in range(FEc):
                    P.emit("pe", lambda e, ic=ic, tb=tb, sb=sb, pp=pp:
                           e.matmul(pp.h[:, 0:512],
                                    sb.h[:, ic * 128:(ic + 1) * 128],
                                    hg2.h[:, ic * T_OWN + tb: ic * T_OWN + tb + 512],
                                    start=(ic == 0), stop=(ic == FEc - 1)),
                           reads=[(sb, ic * 128, (ic + 1) * 128),
                                  (hg2, ic * T_OWN + tb, ic * T_OWN + tb + 512)],
                           writes=[(pp, 0, 512)], tag="w2mm")
                fc = oc // 128
                moff = fc * T_OWN + tb
                P.emit("dve", lambda e, moff=moff, pp=pp: e.tensor_tensor(
                    macc.h[:, moff:moff + 512], macc.h[:, moff:moff + 512],
                    pp.h[:, 0:512], ALU.add),
                    reads=[(pp, 0, 512), (macc, moff, moff + 512)],
                    writes=[(macc, moff, moff + 512)], tag="w2add")

    if stop == "experts":
        return P
    FSc = FS // 128
    for tb in range(0, T_OWN, 512):
        for fs in range(6):
            f0 = fs * 512
            fw = min(512, FS - f0)
            nfc = fw // 128
            dma(P, "pool", (wea, 0, 1 << 58), lambda fw=fw:
                wea.h.ap().rearrange("p (ic o) -> p ic o", o=512)[:, :, 0:fw],
                (ws1, 0, 1 << 58), lambda f0=f0, fw=fw:
                ws1.h.rearrange("(ic p) o -> p ic o", p=128)[:, :, f0:f0 + fw],
                tag=f"ws1_{fs}")
            dma(P, "pool", (web, 0, 1 << 58), lambda fw=fw:
                web.h.ap().rearrange("p (ic o) -> p ic o", o=512)[:, :, 0:fw],
                (ws3, 0, 1 << 58), lambda f0=f0, fw=fw:
                ws3.h.rearrange("(ic p) o -> p ic o", p=128)[:, :, f0:f0 + fw],
                tag=f"ws3_{fs}")
            for fcl in range(nfc):
                ocl = fcl * 128
                pp = ps[fcl % 2]
                for ic in range(FC):
                    P.emit("pe", lambda e, ic=ic, ocl=ocl, tb=tb, pp=pp:
                           e.matmul(pp.h[:, 0:512],
                                    wea.h[:, ic * 512 + ocl: ic * 512 + ocl + 128],
                                    xn3.h[:, ic * T_OWN + tb: ic * T_OWN + tb + 512],
                                    start=(ic == 0), stop=(ic == FC - 1)),
                           reads=[(wea, ic * 512 + ocl, ic * 512 + ocl + 128),
                                  (xn3, ic * T_OWN + tb, ic * T_OWN + tb + 512)],
                           writes=[(pp, 0, 512)], tag="s1mm")
                    P.emit("pe", lambda e, ic=ic, ocl=ocl, tb=tb, pp=pp:
                           e.matmul(ps[2].h[:, 0:512],
                                    web.h[:, ic * 512 + ocl: ic * 512 + ocl + 128],
                                    xn3.h[:, ic * T_OWN + tb: ic * T_OWN + tb + 512],
                                    start=(ic == 0), stop=(ic == FC - 1)),
                           reads=[(web, ic * 512 + ocl, ic * 512 + ocl + 128),
                                  (xn3, ic * T_OWN + tb, ic * T_OWN + tb + 512)],
                           writes=[(ps[2], 0, 512)], tag="s3mm")
                P.emit("act", lambda e, pp=pp: e.activation(
                    ev.h[:, 0:512], pp.h[:, 0:512], AF.Silu),
                    reads=[(pp, 0, 512)], writes=[(ev, 0, 512)], tag="ssilu")
                fcg = fs * 4 + fcl
                P.emit("dve", lambda e, fcg=fcg: e.tensor_tensor(
                    hgs.h[:, fcg * 512:(fcg + 1) * 512], ev.h[:, 0:512],
                    ps[2].h[:, 0:512], ALU.mult),
                    reads=[(ev, 0, 512), (ps[2], 0, 512)],
                    writes=[(hgs, fcg * 512, (fcg + 1) * 512)], tag="shg")
        for oc in range(0, D, 128):
            sb = wea if (oc // 128) % 2 == 0 else web
            dma(P, "pool", (sb, 0, FSc * 128), lambda sb=sb:
                sb.h.ap()[:, 0:FSc * 128]
                .rearrange("p (ic o) -> p ic o", o=128),
                (ws2, 0, 1 << 58), lambda oc=oc:
                ws2.h.rearrange("(ic p) o -> p ic o", p=128)[:, :, oc:oc + 128],
                tag=f"ws2_{oc}")
            pp = ps[4 + (oc // 128) % 2]
            for ic in range(FSc):
                P.emit("pe", lambda e, ic=ic, sb=sb, pp=pp: e.matmul(
                    pp.h[:, 0:512], sb.h[:, ic * 128:(ic + 1) * 128],
                    hgs.h[:, ic * 512:(ic + 1) * 512],
                    start=(ic == 0), stop=(ic == FSc - 1)),
                    reads=[(sb, ic * 128, (ic + 1) * 128),
                           (hgs, ic * 512, (ic + 1) * 512)],
                    writes=[(pp, 0, 512)], tag="ws2mm")
            fc = oc // 128
            moff = fc * T_OWN + tb
            P.emit("dve", lambda e, moff=moff, pp=pp: e.tensor_tensor(
                macc.h[:, moff:moff + 512], macc.h[:, moff:moff + 512],
                pp.h[:, 0:512], ALU.add),
                reads=[(pp, 0, 512), (macc, moff, moff + 512)],
                writes=[(macc, moff, moff + 512)], tag="ws2add")

    if stop == "shared":
        return P
    for tb in range(0, T_OWN, 512):
        dma(P, "sp", (rfin, 0, FC * 512), lambda:
            rfin.h.ap().rearrange("p (fc t) -> p fc t", t=512),
            (r3_d, 0, 1 << 58), lambda tb=tb:
            r3_d.h.rearrange("(fc p) t -> p fc t", p=128)[:, :, tb:tb + 512])
        for fc in range(FC):
            P.emit("dve", lambda e, fc=fc, tb=tb: e.tensor_tensor(
                rfin.h[:, fc * 512:(fc + 1) * 512],
                rfin.h[:, fc * 512:(fc + 1) * 512],
                macc.h[:, fc * T_OWN + tb: fc * T_OWN + tb + 512], ALU.add),
                reads=[(macc, fc * T_OWN + tb, fc * T_OWN + tb + 512),
                       (rfin, fc * 512, (fc + 1) * 512)],
                writes=[(rfin, fc * 512, (fc + 1) * 512)], tag="fin")
        dma(P, "sp", (out_o, 0, 1 << 58), lambda tb=tb:
            out_o.h.rearrange("(fc p) t -> p fc t", p=128)[:, :, tb:tb + 512],
            (rfin, 0, FC * 512), lambda:
            rfin.h.ap().rearrange("p (fc t) -> p fc t", t=512))

    return P


def _build_l2_inputs(inputs, core, res1):
    Kf = np.concatenate([np.asarray(r["k1_o"]) for r in res1], axis=1)
    Vf = np.concatenate([np.asarray(r["v1_o"]) for r in res1], axis=0)
    knall = np.zeros((1, 128), np.float32)
    for c2 in range(N_CORES):
        kn = np.asarray(res1[c2]["kn1_o"]).reshape(-1)
        for h in range(N_HEADS):
            knall[0, h * 8 + c2] = kn[h]
    krot = np.zeros((2, D, T_FULL), BFNP)
    vrot = np.zeros((2, N_HEADS, 128, T_FULL), BFNP)
    flagc = np.zeros((128, 128), np.float32)
    for qc in range(2):
        gqc = 2 * core + qc
        roll = 7680 - gqc * 512
        krot[qc] = np.roll(Kf, roll, axis=1)
        vr = np.roll(Vf, roll, axis=0)
        vrot[qc] = (vr.reshape(64, 128, 16, 128)
                    .transpose(2, 1, 0, 3).reshape(16, 128, 64 * 128))
        for kt in range(64):
            flagc[:, qc * 64 + kt] = 1.0 if kt >= 60 - 4 * gqc else 0.0
    dmsk = np.zeros((4, 128, 512), np.float32)
    i = np.arange(128)[:, None]
    j = np.arange(512)[None, :]
    for k in range(4):
        dmsk[k] = np.where(k * 128 + i > j, np.float32(NEG), np.float32(0.0))
    return {
        "x2_i": np.asarray(res1[core]["x2_o"]),
        "q1_i": np.asarray(res1[core]["q1_o"]),
        "krot": krot,
        "vrot": vrot,
        "qn1_i": np.asarray(res1[core]["qn1_o"]),
        "knall": knall,
        "flagcol": flagc,
        "dmask": dmsk,
        "bo16": _vec16(inputs["L1_attn_bo"]),
        "l1_woT": _fmT(inputs["L1_attn_wo"]),
        "w_router": np.ascontiguousarray(np.asarray(inputs["L1_ffn_w_router"],
                                                    np.float32)),
        "w1": np.ascontiguousarray(np.asarray(inputs["L1_ffn_w1"], np.float32)),
        "w2": np.ascontiguousarray(np.asarray(inputs["L1_ffn_w2"], np.float32)),
        "w3": np.ascontiguousarray(np.asarray(inputs["L1_ffn_w3"], np.float32)),
        "ws1": np.ascontiguousarray(np.asarray(inputs["L1_ffn_ws1"], np.float32)),
        "ws2": np.ascontiguousarray(np.asarray(inputs["L1_ffn_ws2"], np.float32)),
        "ws3": np.ascontiguousarray(np.asarray(inputs["L1_ffn_ws3"], np.float32)),
        "iota8": np.broadcast_to(np.arange(8, dtype=np.float32)[None, :],
                                 (128, 8)).copy(),
        "identf": np.eye(128, dtype=np.float32),
    }


def kernel(**inputs):
    P1 = build_launch1()
    nc1 = bass.Bass("TRN2", target_bir_lowering=False, debug=False)
    P1.build(nc1)
    maps1 = [_build_l1_inputs(inputs, c) for c in range(N_CORES)]
    res1 = _run(nc1, maps1)

    P2 = build_launch2()
    nc2 = bass.Bass("TRN2", target_bir_lowering=False, debug=False)
    P2.build(nc2)
    maps2 = [_build_l2_inputs(inputs, c, res1) for c in range(N_CORES)]
    res2 = _run(nc2, maps2)

    full = np.concatenate([r["out_o"] for r in res2], axis=1)
    return np.ascontiguousarray(full.T)[None].astype(np.float32)


# version 02 (final)
# speedup vs baseline: 1.3077x; 1.3077x over previous
import math

import numpy as np
import ml_dtypes

import concourse.bass as bass
import concourse.mybir as mybir
from concourse.bass_utils import run_bass_kernel_spmd

F32 = mybir.dt.float32
BF16 = mybir.dt.bfloat16
U32 = mybir.dt.uint32
AF = mybir.ActivationFunctionType
ALU = mybir.AluOpType
AX = mybir.AxisListType

N_CORES = 8
D = 2048
FC = 16
T_FULL = 8192
T_OWN = 1024
HALO = 256
G = 16
N_HEADS = 16
HDIM = 128
WINDOW = 256
LN_EPS = 1e-5
D_HID = 8192
NEG = -1e30
SCALE_ATT = 1.0 / math.sqrt(HDIM)
N_EXP = 8
F_EXP = 1408
FS = 2816
TW = HALO + T_OWN
TK = TW + G

BFNP = ml_dtypes.bfloat16



STREAM_SECTION = {"pe": "tensor", "dve": "vector", "act": "scalar",
                  "pool": "gpsimd", "sp": "sync"}


def _dsize(dt):
    return {F32: 4, BF16: 2, U32: 4}[dt]


class _ViewH:

    def __init__(self, buf):
        self.b = buf

    def _base(self):
        b = self.b
        e0 = b.off // 2
        ne = (b.shape[1] * _dsize(b.dtype)) // 2
        ap = b.arena.h[:, e0:e0 + ne]
        if b.dtype != BF16:
            ap = ap.bitcast(b.dtype)
        return ap

    def __getitem__(self, sl):
        return self._base()[sl]

    def ap(self):
        return self._base()


class Buf:
    __slots__ = ("name", "shape", "dtype", "space", "kind", "h", "arena", "off")

    def __init__(self, name, shape, dtype, space, kind=None, arena=None, off=0):
        self.name, self.shape, self.dtype = name, tuple(shape), dtype
        self.space, self.kind = space, kind
        self.arena, self.off = arena, off
        self.h = None

    def key(self):
        return self.arena.name if self.arena is not None else self.name

    def byte_ivl(self, lo, hi):
        ds = _dsize(self.dtype) if self.space != "dram" else 1
        if self.space == "dram":
            return (0, 1 << 58) if hi >= (1 << 57) else (lo, hi)
        hi = min(hi, self.shape[-1])
        return (self.off + lo * ds, self.off + hi * ds)

    def __repr__(self):
        return f"Buf({self.name})"


class Prog:
    def __init__(self, name=""):
        self.name = name
        self.bufs = {}
        self.ops = []
        self.lanes = {}
        self.acc = {}

    def _reg(self, b):
        self.bufs[b.name] = b
        self.acc.setdefault(b.key(), {"w": [], "r": []})
        return b

    def sbuf(self, name, p, f, dtype):
        return self._reg(Buf(name, (p, f), dtype, "sbuf"))

    def view(self, name, p, f, dtype, arena, off):
        assert off % 64 == 0
        assert off + f * _dsize(dtype) <= arena.shape[1] * 2, name
        b = Buf(name, (p, f), dtype, "sbuf", arena=arena, off=off)
        b.h = _ViewH(b)
        return self._reg(b)

    def psum(self, name, p, f, dtype=F32):
        return self._reg(Buf(name, (p, f), dtype, "psum"))

    def dram(self, name, shape, dtype, kind):
        return self._reg(Buf(name, tuple(shape), dtype, "dram", kind))

    def emit(self, stream, fn, reads=(), writes=(), dma=False, tag=""):
        lane = (stream, "d" if dma else "c")
        idx = len(self.ops)
        deps = set()

        def _ivl(spec):
            if isinstance(spec, Buf):
                return (spec.key(),) + spec.byte_ivl(0, 1 << 58)
            b, lo, hi = spec
            return (b.key(),) + b.byte_ivl(lo, hi)

        for spec in reads:
            nm, lo, hi = _ivl(spec)
            for (wl, wh, wi) in self.acc[nm]["w"]:
                if wl < hi and lo < wh:
                    deps.add(wi)
        for spec in writes:
            nm, lo, hi = _ivl(spec)
            a = self.acc[nm]
            for (wl, wh, wi) in a["w"]:
                if wl < hi and lo < wh:
                    deps.add(wi)
            keep = []
            for (rl, rh, ri) in a["r"]:
                if rl < hi and lo < rh:
                    deps.add(ri)
                else:
                    keep.append((rl, rh, ri))
            a["r"] = keep
        for spec in reads:
            nm, lo, hi = _ivl(spec)
            self.acc[nm]["r"].append((lo, hi, idx))
        for spec in writes:
            nm, lo, hi = _ivl(spec)
            a = self.acc[nm]
            neww = []
            for (wl, wh, wi) in a["w"]:
                if wl < hi and lo < wh:
                    if wl < lo:
                        neww.append((wl, lo, wi))
                    if hi < wh:
                        neww.append((hi, wh, wi))
                else:
                    neww.append((wl, wh, wi))
            neww.append((lo, hi, idx))
            a["w"] = neww

        deps.discard(idx)
        lane_ops = self.lanes.setdefault(lane, [])
        op = dict(idx=idx, lane=lane, ord=len(lane_ops), fn=fn, deps=deps,
                  inc=False, tag=tag)
        self.ops.append(op)
        lane_ops.append(idx)
        if dma and op["ord"] >= 8:
            deps.add(lane_ops[op["ord"] - 8])
        for d in list(deps):
            dop = self.ops[d]
            if dop["lane"] == lane and stream == "pe" and not dma:
                deps.discard(d)
            else:
                dop["inc"] = True
        return idx

    def build(self, nc):
        for b in self.bufs.values():
            if b.space == "dram":
                b.h = nc.dram_tensor(b.name, list(b.shape), b.dtype,
                                     kind=b.kind).ap()
        def lkey(op):
            s, k = op["lane"]
            return (s, k) if k == "c" else (s, k, op["ord"] % 8)

        def okey(op):
            return op["ord"] if op["lane"][1] == "c" else op["ord"] // 8

        import contextlib
        stack = contextlib.ExitStack()
        sems = {}
        with stack:
            for b in self.bufs.values():
                if b.space == "sbuf" and b.arena is None:
                    b.h = stack.enter_context(
                        nc.sbuf_tensor(b.name, list(b.shape), b.dtype))
                elif b.space == "psum":
                    b.h = stack.enter_context(
                        nc.psum_tensor(b.name, list(b.shape), b.dtype))
            lkeys = set()
            for lane, idxs in self.lanes.items():
                if lane[1] == "d":
                    for i in idxs:
                        self.ops[i]["inc"] = True
                        lkeys.add(lkey(self.ops[i]))
                if idxs:
                    self.ops[idxs[-1]]["inc"] = True
                    lkeys.add(lkey(self.ops[idxs[-1]]))
            for k in lkeys:
                sems[k] = stack.enter_context(
                    nc.semaphore("sem_" + "_".join(str(x) for x in k)))

            cnt = {}
            for lane, idxs in self.lanes.items():
                if lane[1] != "c":
                    continue
                c, pref = 0, []
                for i in idxs:
                    if self.ops[i]["inc"]:
                        c += 1
                    pref.append(c)
                cnt[lane] = pref

            def wait_val(dop):
                if dop["lane"][1] == "d":
                    return (okey(dop) + 1) * 16
                return cnt[dop["lane"]][dop["ord"]]

            vclock = [None] * len(self.ops)
            for op in self.ops:
                vc = {}
                for d in op["deps"]:
                    dop = self.ops[d]
                    for k, v in vclock[d].items():
                        if vc.get(k, -1) < v:
                            vc[k] = v
                    dk = lkey(dop)
                    if vc.get(dk, -1) < okey(dop):
                        vc[dk] = okey(dop)
                vclock[op["idx"]] = vc

            streams = {}
            for lane, idxs in self.lanes.items():
                streams.setdefault(lane[0], []).extend(idxs)
            block = stack.enter_context(nc.Block())
            for stream, idxs in streams.items():
                idxs.sort()
                section = getattr(block, STREAM_SECTION[stream])

                def run(eng, idxs=idxs):
                    observed = {}
                    for i in idxs:
                        op = self.ops[i]
                        for d in sorted(op["deps"]):
                            dop = self.ops[d]
                            dk = lkey(dop)
                            if dk == lkey(op) and dk == ("pe", "c"):
                                continue
                            if observed.get(dk, -1) >= okey(dop):
                                continue
                            eng.wait_ge(sems[dk], wait_val(dop))
                            for k, v in vclock[d].items():
                                if observed.get(k, -1) < v:
                                    observed[k] = v
                            if observed.get(dk, -1) < okey(dop):
                                observed[dk] = okey(dop)
                        inst = op["fn"](eng)
                        if op["inc"]:
                            k = lkey(op)
                            inst.then_inc(sems[k],
                                          16 if op["lane"][1] == "d" else 1)

                section(run)

            @block.sync
            def _(eng):
                for lane, idxs in self.lanes.items():
                    if not idxs:
                        continue
                    if lane[1] == "c":
                        c = cnt[lane][-1]
                        if c:
                            eng.wait_ge(sems[lane], c)
                    else:
                        n = len(idxs)
                        for sl in range(min(8, n)):
                            last = (n - 1 - sl) // 8 + 1
                            k = (lane[0], lane[1], (n - 1 - sl) % 8)
                            eng.wait_ge(sems[k], last * 16)
        return nc


def dma(P, q, dst, dfn, src, sfn, tag=""):
    P.emit(q, lambda e: e.dma_start(out=dfn(), in_=sfn()),
           reads=[src] if isinstance(src, (tuple, Buf)) else list(src),
           writes=[dst] if isinstance(dst, (tuple, Buf)) else list(dst),
           dma=True, tag=tag)



def build_launch1(debug=False):
    P = Prog("l1")

    x0s = P.dram("x0s", [D, TW], F32, "ExternalInput")
    gfm = P.dram("gfm", [D, G], F32, "ExternalInput")
    w = {}
    for nm in ["l0_wqT", "l0_wkT", "l0_wvT", "l0_woT",
               "l1_wqT", "l1_wkT", "l1_wvT"]:
        w[nm] = P.dram(nm, [D, D], F32, "ExternalInput")
    w["l0_upT"] = P.dram("l0_upT", [D, D_HID], F32, "ExternalInput")
    w["l0_downT"] = P.dram("l0_downT", [D_HID, D], F32, "ExternalInput")
    ln_s = P.dram("ln_s", [128, FC], F32, "ExternalInput")
    ln_b = P.dram("ln_b", [128, FC], F32, "ExternalInput")
    ups = P.dram("ups", [128, 64], F32, "ExternalInput")
    dns = P.dram("dns", [128, FC], F32, "ExternalInput")
    oscal = P.dram("oscal", [1, 1], F32, "ExternalInput")
    bq = P.dram("bq16", [128, FC], F32, "ExternalInput")
    bk = P.dram("bk16", [128, FC], F32, "ExternalInput")
    bv = P.dram("bv16", [128, FC], F32, "ExternalInput")
    m0 = P.dram("m0", [2, 6, 128, 512], F32, "ExternalInput")
    idf = P.dram("identf", [128, 128], F32, "ExternalInput")

    x2_o = P.dram("x2_o", [D, T_OWN], F32, "ExternalOutput")
    q1_o = P.dram("q1_o", [D, T_OWN], BF16, "ExternalOutput")
    k1_o = P.dram("k1_o", [D, T_OWN], BF16, "ExternalOutput")
    v1_o = P.dram("v1_o", [T_OWN, D], BF16, "ExternalOutput")
    qn1_o = P.dram("qn1_o", [1, N_HEADS * T_OWN], F32, "ExternalOutput")
    kn1_o = P.dram("kn1_o", [1, N_HEADS], F32, "ExternalOutput")

    kd = "ExternalOutput" if debug else "Internal"
    xn_d = P.dram("xn_d", [D, TK], BF16, kd)
    q0_d = P.dram("q0_d", [D, T_OWN], BF16, kd)
    k0_d = P.dram("k0_d", [D, TK], BF16, kd)
    v0_d = P.dram("v0_d", [TK, D], BF16, kd)
    r01_d = P.dram("r01_d", [D, T_OWN], F32, "Internal")
    upTs_d = P.dram("upTs_d", [D, D_HID], BF16, "Internal")
    dnTs_d = P.dram("dnTs_d", [D_HID, D], BF16, "Internal")
    attn_d = P.dram("attn_d", [D, T_OWN], BF16, kd)
    r1_d = P.dram("r1_d", [D, T_OWN], F32, kd)
    xn2_d = P.dram("xn2_d", [D, T_OWN], BF16, kd)

    AR = P.sbuf("AR", 128, 88 * 1024, BF16)

    ev = P.sbuf("ev", 128, 512, BF16)
    evr = [P.sbuf(f"evr{i}", 128, 512, BF16) for i in range(4)]
    ev2 = P.sbuf("ev2", 128, 512, F32)
    bc2 = P.sbuf("bc2", 128, 512, F32)
    rows = P.sbuf("rows", 1, 3104, F32)
    rowbf = P.sbuf("rowbf", 1, 512, BF16)
    ones_f = P.sbuf("ones_f", 128, 1, F32)
    ones_b = P.sbuf("ones_b", 128, 1, BF16)
    onesr_f = P.sbuf("onesr_f", 1, 128, F32)
    onesr_b = P.sbuf("onesr_b", 1, 128, BF16)
    lnsv = P.sbuf("lnsv", 128, FC, F32)
    lnbv = P.sbuf("lnbv", 128, FC, F32)
    upsv = P.sbuf("upsv", 128, 64, F32)
    dnsv = P.sbuf("dnsv", 128, FC, F32)
    dnscmb = P.sbuf("dnscmb", 128, FC, F32)
    osc = P.sbuf("osc", 1, 1, F32)
    bqv = P.sbuf("bqv", 128, FC, F32)
    bkv = P.sbuf("bkv", 128, FC, F32)
    bvv = P.sbuf("bvv", 128, FC, F32)
    bvrow = P.sbuf("bvrow", 1, D, BF16)
    idf_sb = P.sbuf("idf_sb", 128, 128, F32)

    knrow = P.sbuf("knrow", 1, 64, F32)
    eps1 = P.sbuf("eps1", 1, 1, F32)
    kn1r = P.sbuf("kn1r", 1, N_HEADS, F32)

    KB = 1024
    x0b = P.view("x0b", 128, FC * 512, F32, AR, 0)
    sqv = P.view("sqv", 128, 8 * 512, F32, AR, 64 * KB)
    ln0b = P.view("ln0b", 128, FC * 512, BF16, AR, 80 * KB)
    xnb = P.view("xnb", 128, FC * 512, BF16, AR, 96 * KB)
    ws0 = P.view("ws0", 128, FC * 512, BF16, AR, 0)
    ws1b = P.view("ws1b", 128, FC * 512, BF16, AR, 16 * KB)
    m0b = P.view("m0b", 128, 2 * 6 * 512, BF16, AR, 32 * KB)
    pt = P.view("pt", 128, 7 * 512, BF16, AR, 44 * KB)
    qh = P.view("qh", 128, 512, BF16, AR, 51 * KB)
    kwin = P.view("kwin", 128, 768 + G, BF16, AR, 52 * KB)
    vwin = P.view("vwin", 128, 7 * 128, BF16, AR, 54 * KB)
    rb = P.view("rb", 128, 512, F32, AR, 56 * KB)
    ablk = P.view("ablk", 128, FC * 512, BF16, AR, 44 * KB)
    rblk = P.view("rblk", 128, FC * 512, F32, AR, 112 * KB)
    xn1 = P.view("xn1", 128, FC * 512, BF16, AR, 0)
    ws5a = P.view("ws5a", 128, FC * 512, BF16, AR, 16 * KB)
    ws5b = P.view("ws5b", 128, FC * 512, BF16, AR, 32 * KB)
    sq5 = P.view("sq5", 128, 8 * 512, F32, AR, 48 * KB)
    hbuf = P.view("hbuf", 128, 64 * 512, BF16, AR, 64 * KB)
    rblk5 = P.view("rblk5", 128, FC * 512, F32, AR, 128 * KB)
    xn2b = P.view("xn2b", 128, FC * 512, BF16, AR, 160 * KB)
    ss0 = P.view("ss0", 128, FC * 512, BF16, AR, 112 * KB)
    ss1 = P.view("ss1", 128, FC * 512, BF16, AR, 128 * KB)
    xnb2 = P.view("xnb2", 128, FC * 512, BF16, AR, 112 * KB)

    ps = [P.psum(f"ps{i}", 128, 512) for i in range(6)]
    ps_a = P.psum("ps_a", 128, 512)
    ps_b = P.psum("ps_b", 128, 512)

    P.emit("dve", lambda e: e.memset(eps1.h[:], LN_EPS), writes=[eps1])
    P.emit("dve", lambda e: e.memset(ones_f.h[:], 1.0), writes=[ones_f])
    P.emit("dve", lambda e: e.memset(ones_b.h[:], 1.0), writes=[ones_b])
    P.emit("dve", lambda e: e.memset(onesr_f.h[:], 1.0), writes=[onesr_f])
    P.emit("dve", lambda e: e.memset(onesr_b.h[:], 1.0), writes=[onesr_b])
    for (buf, src) in [(lnsv, ln_s), (lnbv, ln_b), (upsv, ups), (dnsv, dns),
                       (osc, oscal), (bqv, bq), (bkv, bk), (bvv, bv),
                       (idf_sb, idf)]:
        dma(P, "sp", (buf, 0, 1 << 58), lambda b=buf: b.h[:],
            (src, 0, 1 << 58), lambda s=src: s.h[:])
    P.emit("pe", lambda e: e.matmul(ps_a.h[:, 0:1], onesr_f.h[:], osc.h[:],
                                    start=True, stop=True),
           reads=[onesr_f, osc], writes=[(ps_a, 0, 1)])
    P.emit("dve", lambda e: e.tensor_copy(ev2.h[:, 0:1], ps_a.h[:, 0:1]),
           reads=[(ps_a, 0, 1)], writes=[(ev2, 0, 1)])
    P.emit("dve", lambda e: e.tensor_scalar(dnscmb.h[:], dnsv.h[:],
                                            ev2.h[:, 0:1], None, ALU.mult),
           reads=[dnsv, (ev2, 0, 1)], writes=[dnscmb])
    for fc in range(FC):
        P.emit("pe", lambda e, fc=fc: e.transpose(
            ps_a.h[0:1, 0:128], bvv.h[:, fc:fc + 1], idf_sb.h[:]),
            reads=[bvv, idf_sb], writes=[(ps_a, 0, 128)])
        P.emit("dve", lambda e, fc=fc: e.tensor_copy(
            bvrow.h[0:1, fc * 128:(fc + 1) * 128], ps_a.h[0:1, 0:128]),
            reads=[(ps_a, 0, 128)], writes=[(bvrow, fc * 128, (fc + 1) * 128)])

    sgi = 0
    for (wname, dst, DI, DO) in [("l0_upT", upTs_d, D, D_HID),
                                 ("l0_downT", dnTs_d, D_HID, D)]:
        for ob in range(0, DO, 512):
            for icb in range(0, DI // 128, 16):
                sb = ss0 if sgi % 2 == 0 else ss1
                sgi += 1
                dma(P, "pool", (sb, 0, 1 << 58), lambda sb=sb:
                    sb.h.ap().rearrange("p (ic o) -> p ic o", o=512),
                    (w[wname], 0, 1 << 58),
                    lambda wname=wname, ob=ob, icb=icb:
                    w[wname].h.rearrange("(ic p) o -> p ic o", p=128)
                    [:, icb:icb + 16, ob:ob + 512], tag=f"sg{wname}{ob}")
                P.emit("act", lambda e, sb=sb: e.activation(sb.h[:], sb.h[:],
                                                            AF.Sign),
                       reads=[(sb, 0, FC * 512)], writes=[(sb, 0, FC * 512)],
                       tag="sgact")
                dma(P, "sp", (dst, 0, 1 << 58),
                    lambda dst=dst, ob=ob, icb=icb:
                    dst.h.rearrange("(ic p) o -> p ic o", p=128)
                    [:, icb:icb + 16, ob:ob + 512],
                    (sb, 0, FC * 512), lambda sb=sb:
                    sb.h.ap().rearrange("p (ic o) -> p ic o", o=512))

    def ln_block(src, TB, SRCW, sq, SQW, dst, affine, tag):
        nT = TB
        for half in range(2):
            f0 = half * 8
            P.emit("act", lambda e, f0=f0: e.activation(
                sq.h.ap().rearrange("p (fc t) -> p fc t", t=SQW)[:, :, 0:nT],
                src.h.ap().rearrange("p (fc t) -> p fc t", t=SRCW)
                [:, f0:f0 + 8, 0:nT], AF.Square),
                reads=[(src, f0 * SRCW, (f0 + 8) * SRCW)],
                writes=[(sq, 0, 8 * SQW)], tag=tag + "sq")
            for fc in range(8):
                fcg = f0 + fc
                P.emit("pe", lambda e, fc=fc, fcg=fcg: e.matmul(
                    ps_b.h[0:1, 0:nT], ones_f.h[:],
                    sq.h[:, fc * SQW: fc * SQW + nT],
                    start=(fcg == 0), stop=(fcg == FC - 1)),
                    reads=[ones_f, (sq, fc * SQW, fc * SQW + nT)],
                    writes=[(ps_b, 0, nT)], tag=tag + "ssq")
        for fc in range(FC):
            P.emit("pe", lambda e, fc=fc: e.matmul(
                ps_a.h[0:1, 0:nT], ones_f.h[:],
                src.h[:, fc * SRCW: fc * SRCW + nT],
                start=(fc == 0), stop=(fc == FC - 1)),
                reads=[ones_f, (src, fc * SRCW, fc * SRCW + nT)],
                writes=[(ps_a, 0, nT)], tag=tag + "ssum")
        MU, RST, NMU = 0, 512, 1024
        P.emit("dve", lambda e: e.tensor_single_scalar(
            rows.h[0:1, MU:MU + nT], ps_a.h[0:1, 0:nT], 1.0 / D, ALU.mult),
            reads=[(ps_a, 0, nT)], writes=[(rows, MU, MU + nT)])
        P.emit("dve", lambda e: e.tensor_single_scalar(
            rows.h[0:1, RST:RST + nT], ps_b.h[0:1, 0:nT], 1.0 / D, ALU.mult),
            reads=[(ps_b, 0, nT)], writes=[(rows, RST, RST + nT)])
        P.emit("dve", lambda e: e.tensor_tensor(
            rows.h[0:1, NMU:NMU + nT], rows.h[0:1, MU:MU + nT],
            rows.h[0:1, MU:MU + nT], ALU.mult),
            reads=[(rows, MU, MU + nT)], writes=[(rows, NMU, NMU + nT)])
        P.emit("dve", lambda e: e.tensor_tensor(
            rows.h[0:1, RST:RST + nT], rows.h[0:1, RST:RST + nT],
            rows.h[0:1, NMU:NMU + nT], ALU.subtract),
            reads=[(rows, RST, RST + nT), (rows, NMU, NMU + nT)],
            writes=[(rows, RST, RST + nT)])
        P.emit("act", lambda e: e.activation(
            rows.h[0:1, RST:RST + nT], rows.h[0:1, RST:RST + nT], AF.Sqrt,
            bias=eps1.h[:]), reads=[(rows, RST, RST + nT), eps1],
            writes=[(rows, RST, RST + nT)])
        P.emit("dve", lambda e: e.reciprocal(
            rows.h[0:1, RST:RST + nT], rows.h[0:1, RST:RST + nT]),
            reads=[(rows, RST, RST + nT)], writes=[(rows, RST, RST + nT)])
        P.emit("dve", lambda e: e.scalar_tensor_tensor(
            rows.h[0:1, NMU:NMU + nT], rows.h[0:1, MU:MU + nT], -1.0,
            rows.h[0:1, RST:RST + nT], ALU.mult, ALU.mult),
            reads=[(rows, MU, MU + nT), (rows, RST, RST + nT)],
            writes=[(rows, NMU, NMU + nT)])
        P.emit("pe", lambda e: e.matmul(
            ps_a.h[:, 0:nT], onesr_f.h[:], rows.h[0:1, RST:RST + nT],
            start=True, stop=True),
            reads=[onesr_f, (rows, RST, RST + nT)], writes=[(ps_a, 0, nT)])
        P.emit("pe", lambda e: e.matmul(
            ps_b.h[:, 0:nT], onesr_f.h[:], rows.h[0:1, NMU:NMU + nT],
            start=True, stop=True),
            reads=[onesr_f, (rows, NMU, NMU + nT)], writes=[(ps_b, 0, nT)])
        P.emit("dve", lambda e: e.tensor_copy(ev2.h[:, 0:nT], ps_a.h[:, 0:nT]),
               reads=[(ps_a, 0, nT)], writes=[(ev2, 0, nT)])
        P.emit("dve", lambda e: e.tensor_copy(bc2.h[:, 0:nT], ps_b.h[:, 0:nT]),
               reads=[(ps_b, 0, nT)], writes=[(bc2, 0, nT)])
        for fc in range(FC):
            dtgt, dlo = dst(fc)
            P.emit("dve", lambda e, fc=fc: e.tensor_tensor(
                sq.h[:, 0:nT], src.h[:, fc * SRCW:fc * SRCW + nT],
                ev2.h[:, 0:nT], ALU.mult),
                reads=[(src, fc * SRCW, fc * SRCW + nT), (ev2, 0, nT)],
                writes=[(sq, 0, nT)], tag=tag + "ap1")
            if affine is None:
                P.emit("dve", lambda e, dtgt=dtgt, dlo=dlo: e.tensor_tensor(
                    dtgt.h[:, dlo:dlo + nT], sq.h[:, 0:nT],
                    bc2.h[:, 0:nT], ALU.add),
                    reads=[(sq, 0, nT), (bc2, 0, nT)],
                    writes=[(dtgt, dlo, dlo + nT)], tag=tag + "ap2")
            else:
                asc, abi = affine
                P.emit("dve", lambda e: e.tensor_tensor(
                    sq.h[:, SQW:SQW + nT], sq.h[:, 0:nT],
                    bc2.h[:, 0:nT], ALU.add),
                    reads=[(sq, 0, nT), (bc2, 0, nT)],
                    writes=[(sq, SQW, SQW + nT)], tag=tag + "ap2")
                P.emit("act", lambda e, fc=fc, dtgt=dtgt, dlo=dlo:
                       e.activation(dtgt.h[:, dlo:dlo + nT],
                                    sq.h[:, SQW:SQW + nT], AF.Identity,
                                    bias=abi.h[:, fc:fc + 1],
                                    scale=asc.h[:, fc:fc + 1]),
                       reads=[(sq, SQW, SQW + nT), asc, abi],
                       writes=[(dtgt, dlo, dlo + nT)], tag=tag + "ap3")

    for (b0, b1) in [(0, 512), (512, 1024), (1024, 1280)]:
        TB = b1 - b0
        dma(P, "sp", (x0b, 0, FC * 512), lambda TB=TB:
            x0b.h.ap().rearrange("p (fc t) -> p fc t", t=512)[:, :, 0:TB],
            (x0s, 0, 1 << 58), lambda b0=b0, b1=b1:
            x0s.h.rearrange("(fc p) t -> p fc t", p=128)[:, :, b0:b1])
        ln_block(x0b, TB, 512, sqv, 512, lambda fc: (ln0b, fc * 512), None,
                 f"ln0_{b0}_")
        o0 = max(b0, HALO)
        if o0 < b1:
            for half in range(2):
                f0 = half * 8
                for fc in range(f0, f0 + 8):
                    P.emit("dve", lambda e, fc=fc, f0=f0, o0=o0, b1=b1, b0=b0:
                           e.tensor_tensor(
                               sqv.h[:, (fc - f0) * 512:
                                     (fc - f0) * 512 + b1 - o0],
                               x0b.h[:, fc * 512 + o0 - b0: fc * 512 + b1 - b0],
                               ln0b.h[:, fc * 512 + o0 - b0: fc * 512 + b1 - b0],
                               ALU.add),
                           reads=[(x0b, fc * 512 + o0 - b0, fc * 512 + b1 - b0),
                                  (ln0b, fc * 512 + o0 - b0, fc * 512 + b1 - b0)],
                           writes=[(sqv, (fc - f0) * 512,
                                    (fc - f0) * 512 + b1 - o0)],
                           tag="r01")
                dma(P, "sp", (r01_d, 0, 1 << 58),
                    lambda f0=f0, o0=o0, b1=b1:
                    r01_d.h.rearrange("(fc p) t -> p fc t", p=128)
                    [:, f0:f0 + 8, o0 - HALO:b1 - HALO],
                    (sqv, 0, 8 * 512), lambda o0=o0, b1=b1:
                    sqv.h.ap().rearrange("p (fc t) -> p fc t", t=512)
                    [:, :, 0:b1 - o0])
        P.emit("act", lambda e, TB=TB: e.activation(
            x0b.h.ap().rearrange("p (fc t) -> p fc t", t=512)[:, :, 0:TB],
            ln0b.h.ap().rearrange("p (fc t) -> p fc t", t=512)[:, :, 0:TB],
            AF.Identity), reads=[(ln0b, 0, FC * 512)],
            writes=[(x0b, 0, FC * 512)], tag="cp32")
        ln_block(x0b, TB, 512, sqv, 512, lambda fc: (xnb, fc * 512),
                 (lnsv, lnbv), f"lni_{b0}_")
        dma(P, "sp", (xn_d, 0, 1 << 58), lambda b0=b0, b1=b1:
            xn_d.h.rearrange("(fc p) t -> p fc t", p=128)[:, :, b0:b1],
            (xnb, 0, FC * 512), lambda TB=TB:
            xnb.h.ap().rearrange("p (fc t) -> p fc t", t=512)[:, :, 0:TB])
    dma(P, "pool", (xn_d, 0, 1 << 58), lambda:
        xn_d.h.rearrange("(fc p) t -> p fc t", p=128)[:, :, TW:TK],
        (gfm, 0, 1 << 58), lambda:
        gfm.h.rearrange("(fc p) t -> p fc t", p=128))

    def load_wslab(wname, o0, ow, sb, tag=""):
        dma(P, "pool", (sb, 0, 1 << 58), lambda sb=sb, ow=ow:
            sb.h.ap().rearrange("p (ic o) -> p ic o", o=512)[:, :, 0:ow],
            (w[wname], 0, 1 << 58), lambda wname=wname, o0=o0, ow=ow:
            w[wname].h.rearrange("(ic p) o -> p ic o", p=128)[:, :, o0:o0 + ow],
            tag=tag)

    def load_xnb(src_d, t0, TB, xb, tag=""):
        dma(P, "sp", (xb, 0, FC * 512), lambda TB=TB, xb=xb:
            xb.h.ap().rearrange("p (fc t) -> p fc t", t=512)[:, :, 0:TB],
            (src_d, 0, 1 << 58), lambda src_d=src_d, t0=t0, TB=TB:
            src_d.h.rearrange("(fc p) t -> p fc t", p=128)[:, :, t0:t0 + TB],
            tag=tag)

    def proj_fm(wname, src_d, t0, t1, out_cb, tag=""):
        DO = w[wname].shape[1]
        ti = 0
        for ob in range(0, DO, 512):
            sb = ws0 if (ob // 512) % 2 == 0 else ws1b
            load_wslab(wname, ob, 512, sb, tag=tag + f"w{ob}")
            for tb in range(t0, t1, 512):
                TB = min(512, t1 - tb)
                xb = xnb if ti % 2 == 0 else xnb2
                ti += 1
                load_xnb(src_d, tb, TB, xb, tag=tag + "x")
                for oc in range(ob, ob + 512, 128):
                    ocl = oc - ob
                    pp = ps[(oc // 128) % 2]
                    for ic in range(FC):
                        P.emit("pe", lambda e, ic=ic, ocl=ocl, TB=TB, sb=sb, pp=pp, xb=xb:
                               e.matmul(pp.h[:, 0:TB],
                                        sb.h[:, ic * 512 + ocl: ic * 512 + ocl + 128],
                                        xb.h[:, ic * 512: ic * 512 + TB],
                                        start=(ic == 0), stop=(ic == FC - 1)),
                               reads=[(sb, ic * 512 + ocl, ic * 512 + ocl + 128),
                                      (xb, ic * 512, ic * 512 + TB)],
                               writes=[(pp, 0, TB)], tag=tag + "mm")
                    out_cb(oc, tb, TB, pp)

    evc = [0]

    def evac_to_dram(dst, bias, dt0=0):
        def cb(oc, tb, TB, pp):
            eb = evr[evc[0] % 4]
            evc[0] += 1
            if bias is None:
                P.emit("act", lambda e, TB=TB, pp=pp, eb=eb: e.activation(
                    eb.h[:, 0:TB], pp.h[:, 0:TB], AF.Identity),
                    reads=[(pp, 0, TB)], writes=[(eb, 0, TB)])
            else:
                P.emit("act", lambda e, oc=oc, TB=TB, pp=pp, eb=eb: e.activation(
                    eb.h[:, 0:TB], pp.h[:, 0:TB], AF.Identity,
                    bias=bias.h[:, oc // 128:oc // 128 + 1]),
                    reads=[(pp, 0, TB), bias], writes=[(eb, 0, TB)])
            dma(P, "sp", (dst, 0, 1 << 58),
                lambda dst=dst, oc=oc, tb=tb, TB=TB, eb=eb:
                dst.h[oc:oc + 128, tb - dt0:tb - dt0 + TB],
                (eb, 0, TB), lambda TB=TB, eb=eb: eb.h[:, 0:TB])
        return cb

    proj_fm("l0_wqT", xn_d, HALO, HALO + T_OWN,
            evac_to_dram(q0_d, None, dt0=HALO), tag="q0")
    proj_fm("l0_wkT", xn_d, 0, TK, evac_to_dram(k0_d, None), tag="k0")

    def proj_tm(wname, src_d, t0, t1, dst, brow, tag=""):
        DO = w[wname].shape[1]
        ti = 0
        for ob in range(0, DO, 512):
            sb = ws0 if (ob // 512) % 2 == 0 else ws1b
            load_wslab(wname, ob, 512, sb, tag=tag + f"w{ob}")
            for tb in range(t0, t1, 512):
                TB = min(512, t1 - tb)
                xb = xnb if ti % 2 == 0 else xnb2
                ti += 1
                load_xnb(src_d, tb, TB, xb, tag=tag + "x")
                for tt in range(0, TB, 128):
                    tw = min(128, TB - tt)
                    pp = ps[(tt // 128) % 2]
                    for ic in range(FC):
                        P.emit("pe", lambda e, ic=ic, tt=tt, tw=tw, sb=sb, pp=pp, xb=xb:
                               e.matmul(pp.h[0:tw, 0:512],
                                        xb.h[:, ic * 512 + tt: ic * 512 + tt + tw],
                                        sb.h[:, ic * 512:(ic + 1) * 512],
                                        start=(ic == 0), stop=(ic == FC - 1)),
                               reads=[(sb, ic * 512, (ic + 1) * 512),
                                      (xb, ic * 512 + tt, ic * 512 + tt + tw)],
                               writes=[(pp, 0, 512)], tag=tag + "mm")
                    if brow is not None:
                        P.emit("pe", lambda e, tw=tw, pp=pp, ob=ob:
                               e.matmul(pp.h[0:tw, 0:512],
                                        onesr_b.h[0:1, 0:tw],
                                        brow.h[0:1, ob:ob + 512],
                                        start=False, stop=True),
                               reads=[onesr_b, (brow, ob, ob + 512)],
                               writes=[(pp, 0, 512)], tag=tag + "bf")
                    eb = evr[evc[0] % 4]
                    evc[0] += 1
                    P.emit("act", lambda e, tw=tw, pp=pp, eb=eb: e.activation(
                        eb.h[0:tw, 0:512], pp.h[0:tw, 0:512], AF.Identity),
                        reads=[(pp, 0, 512)], writes=[(eb, 0, 512)])
                    dma(P, "sp", (dst, 0, 1 << 58),
                        lambda dst=dst, tb=tb, tt=tt, tw=tw, ob=ob, eb=eb:
                        dst.h[tb + tt:tb + tt + tw, ob:ob + 512],
                        (eb, 0, 512), lambda tw=tw, eb=eb: eb.h[0:tw, 0:512])

    proj_tm("l0_wvT", xn_d, 0, TK, v0_d, None, tag="v0")

    QN, NB, RR = 1536, 2048, 2560
    dma(P, "pool", (m0b, 0, 1 << 58), lambda:
        m0b.h.ap().rearrange("p (v r t) -> p v r t", r=6, t=512),
        (m0, 0, 1 << 58), lambda: m0.h.rearrange("v r p t -> p v r t"))

    def l0_attn(h, qc):
        t0 = qc * 512
        dma(P, "sp", (qh, 0, 512), lambda: qh.h[:],
            (q0_d, 0, 1 << 58), lambda h=h, t0=t0:
            q0_d.h[h * 128:(h + 1) * 128, t0:t0 + 512])
        dma(P, "sp", (kwin, 0, 768), lambda: kwin.h[:, 0:768],
            (k0_d, 0, 1 << 58), lambda h=h, t0=t0:
            k0_d.h[h * 128:(h + 1) * 128, t0:t0 + 768])
        dma(P, "sp", (kwin, 768, 768 + G), lambda: kwin.h[:, 768:768 + G],
            (k0_d, 0, 1 << 58), lambda h=h:
            k0_d.h[h * 128:(h + 1) * 128, TW:TK])
        dma(P, "sp", (vwin, 0, 6 * 128), lambda:
            vwin.h.ap().rearrange("p (kc d) -> p kc d", d=128)[:, 0:6, :],
            (v0_d, 0, 1 << 58), lambda h=h, t0=t0:
            v0_d.h[t0:t0 + 768, h * 128:(h + 1) * 128]
            .rearrange("(kc p) d -> p kc d", p=128))
        dma(P, "sp", (vwin, 6 * 128, 6 * 128 + G), lambda:
            vwin.h[0:G, 6 * 128:7 * 128],
            (v0_d, 0, 1 << 58), lambda h=h:
            v0_d.h[TW:TK, h * 128:(h + 1) * 128])
        P.emit("act", lambda e: e.activation(sqv.h[:, 0:512], qh.h[:],
                                             AF.Square),
               reads=[(qh, 0, 512)], writes=[(sqv, 0, 512)])
        P.emit("pe", lambda e: e.matmul(ps_a.h[0:1, 0:512], ones_f.h[:],
                                        sqv.h[:, 0:512], start=True, stop=True),
               reads=[ones_f, (sqv, 0, 512)], writes=[(ps_a, 0, 512)])
        P.emit("act", lambda e: e.activation(rows.h[0:1, QN:QN + 512],
                                             ps_a.h[0:1, 0:512], AF.Sqrt),
               reads=[(ps_a, 0, 512)], writes=[(rows, QN, QN + 512)])
        P.emit("act", lambda e: e.activation(sqv.h[:, 512:512 + 784],
                                             kwin.h[:, 0:784], AF.Square),
               reads=[(kwin, 0, 784)], writes=[(sqv, 512, 512 + 784)])
        P.emit("pe", lambda e: e.matmul(ps_a.h[0:1, 0:512], ones_f.h[:],
                                        sqv.h[:, 512:1024],
                                        start=True, stop=True),
               reads=[ones_f, (sqv, 512, 1024)], writes=[(ps_a, 0, 512)])
        P.emit("pe", lambda e: e.matmul(ps_b.h[0:1, 0:272], ones_f.h[:],
                                        sqv.h[:, 1024:1296],
                                        start=True, stop=True),
               reads=[ones_f, (sqv, 1024, 1296)], writes=[(ps_b, 0, 272)])
        P.emit("dve", lambda e: e.reduce_max(rows.h[0:1, RR:RR + 1],
                                             ps_a.h[0:1, 0:512], axis=AX.X),
               reads=[(ps_a, 0, 512)], writes=[(rows, RR, RR + 1)])
        P.emit("dve", lambda e: e.reduce_max(rows.h[0:1, RR + 1:RR + 2],
                                             ps_b.h[0:1, 0:272], axis=AX.X),
               reads=[(ps_b, 0, 272)], writes=[(rows, RR + 1, RR + 2)])
        P.emit("dve", lambda e: e.tensor_tensor(
            rows.h[0:1, RR:RR + 1], rows.h[0:1, RR:RR + 1],
            rows.h[0:1, RR + 1:RR + 2], ALU.max),
            reads=[(rows, RR, RR + 2)], writes=[(rows, RR, RR + 1)])
        P.emit("act", lambda e: e.activation(rows.h[0:1, RR:RR + 1],
                                             rows.h[0:1, RR:RR + 1], AF.Sqrt),
               reads=[(rows, RR, RR + 1)], writes=[(rows, RR, RR + 1)])
        P.emit("dve", lambda e: e.tensor_scalar(
            rows.h[0:1, NB:NB + 512], rows.h[0:1, QN:QN + 512],
            rows.h[0:1, RR:RR + 1], -1.0, ALU.mult, ALU.mult),
            reads=[(rows, QN, QN + 512), (rows, RR, RR + 1)],
            writes=[(rows, NB, NB + 512)])
        P.emit("act", lambda e: e.activation(rowbf.h[0:1, 0:512],
                                             rows.h[0:1, NB:NB + 512],
                                             AF.Identity),
               reads=[(rows, NB, NB + 512)], writes=[(rowbf, 0, 512)])
        for kt in range(6):
            pp = ps[2 + (kt % 2)]
            P.emit("pe", lambda e, kt=kt, pp=pp: e.matmul(
                pp.h[:, 0:512], kwin.h[:, kt * 128:(kt + 1) * 128], qh.h[:],
                start=True, stop=False),
                reads=[(kwin, kt * 128, (kt + 1) * 128), (qh, 0, 512)],
                writes=[(pp, 0, 512)], tag="l0st")
            P.emit("pe", lambda e, pp=pp: e.matmul(
                pp.h[:, 0:512], onesr_b.h[:], rowbf.h[0:1, 0:512],
                start=False, stop=True),
                reads=[onesr_b, (rowbf, 0, 512)], writes=[(pp, 0, 512)])
            moff = (qc * 6 + kt) * 512
            P.emit("dve", lambda e, pp=pp, moff=moff: e.tensor_tensor(
                pp.h[:, 0:512], pp.h[:, 0:512], m0b.h[:, moff:moff + 512],
                ALU.add),
                reads=[(pp, 0, 512), (m0b, moff, moff + 512)],
                writes=[(pp, 0, 512)], tag="l0msk")
            P.emit("act", lambda e, kt=kt, pp=pp: e.activation(
                pt.h[:, kt * 512:(kt + 1) * 512], pp.h[:, 0:512], AF.Exp,
                scale=SCALE_ATT),
                reads=[(pp, 0, 512)],
                writes=[(pt, kt * 512, (kt + 1) * 512)], tag="l0exp")
        ppg = ps[4]
        P.emit("pe", lambda e, ppg=ppg: e.matmul(
            ppg.h[0:G, 0:512], kwin.h[:, 768:768 + G], qh.h[:],
            start=True, stop=False),
            reads=[(kwin, 768, 768 + G), (qh, 0, 512)],
            writes=[(ppg, 0, 512)])
        P.emit("pe", lambda e, ppg=ppg: e.matmul(
            ppg.h[0:G, 0:512], onesr_b.h[0:1, 0:G], rowbf.h[0:1, 0:512],
            start=False, stop=True),
            reads=[onesr_b, (rowbf, 0, 512)], writes=[(ppg, 0, 512)])
        P.emit("act", lambda e, ppg=ppg: e.activation(
            pt.h[0:G, 6 * 512:7 * 512], ppg.h[0:G, 0:512], AF.Exp,
            scale=SCALE_ATT),
            reads=[(ppg, 0, 512)], writes=[(pt, 6 * 512, 7 * 512)],
            tag="l0exp")
        for kt in range(7):
            krows = G if kt == 6 else 128
            rowsrc = (pt, kt * 512, (kt + 1) * 512)
            P.emit("pe", lambda e, kt=kt, krows=krows: e.matmul(
                ps_a.h[0:1, 0:512], ones_b.h[0:krows, :],
                pt.h[0:krows, kt * 512:(kt + 1) * 512],
                start=(kt == 0), stop=(kt == 6)),
                reads=[ones_b, rowsrc], writes=[(ps_a, 0, 512)], tag="l0sum")
            P.emit("pe", lambda e, kt=kt, krows=krows: e.matmul(
                ps[5].h[:, 0:512],
                vwin.h[0:krows, kt * 128:(kt + 1) * 128],
                pt.h[0:krows, kt * 512:(kt + 1) * 512],
                start=(kt == 0), stop=(kt == 6)),
                reads=[(vwin, kt * 128, (kt + 1) * 128), rowsrc],
                writes=[(ps[5], 0, 512)], tag="l0pv")
        P.emit("dve", lambda e: e.reciprocal(rows.h[0:1, RR + 2:RR + 514],
                                             ps_a.h[0:1, 0:512]),
               reads=[(ps_a, 0, 512)], writes=[(rows, RR + 2, RR + 514)])
        P.emit("pe", lambda e: e.matmul(ps_b.h[:, 0:512], onesr_f.h[:],
                                        rows.h[0:1, RR + 2:RR + 514],
                                        start=True, stop=True),
               reads=[onesr_f, (rows, RR + 2, RR + 514)],
               writes=[(ps_b, 0, 512)])
        P.emit("dve", lambda e: e.tensor_copy(rb.h[:], ps_b.h[:, 0:512]),
               reads=[(ps_b, 0, 512)], writes=[rb])
        P.emit("dve", lambda e: e.tensor_tensor(
            ev.h[:, 0:512], ps[5].h[:, 0:512], rb.h[:], ALU.mult),
            reads=[(ps[5], 0, 512), rb], writes=[(ev, 0, 512)], tag="l0nrm")
        dma(P, "sp", (attn_d, 0, 1 << 58), lambda h=h, t0=t0:
            attn_d.h[h * 128:(h + 1) * 128, t0:t0 + 512],
            (ev, 0, 512), lambda: ev.h[:, 0:512])

    for h in range(N_HEADS):
        for qc in range(2):
            l0_attn(h, qc)

    for tb in range(0, T_OWN, 512):
        dma(P, "sp", (rblk, 0, FC * 512), lambda:
            rblk.h.ap().rearrange("p (fc t) -> p fc t", t=512),
            (r01_d, 0, 1 << 58), lambda tb=tb:
            r01_d.h.rearrange("(fc p) t -> p fc t", p=128)[:, :, tb:tb + 512])
        for ob in range(0, D, 512):
            sb = ws0 if (ob // 512) % 2 == 0 else ws1b
            load_wslab("l0_woT", ob, 512, sb, tag=f"wo{ob}")
            dma(P, "sp", (ablk, 0, FC * 512), lambda:
                ablk.h.ap().rearrange("p (fc t) -> p fc t", t=512),
                (attn_d, 0, 1 << 58), lambda tb=tb:
                attn_d.h.rearrange("(fc p) t -> p fc t", p=128)
                [:, :, tb:tb + 512])
            for oc in range(ob, ob + 512, 128):
                ocl = oc - ob
                pp = ps[(oc // 128) % 2]
                for ic in range(FC):
                    P.emit("pe", lambda e, ic=ic, ocl=ocl, sb=sb, pp=pp:
                           e.matmul(pp.h[:, 0:512],
                                    sb.h[:, ic * 512 + ocl: ic * 512 + ocl + 128],
                                    ablk.h[:, ic * 512:(ic + 1) * 512],
                                    start=(ic == 0), stop=(ic == FC - 1)),
                           reads=[(sb, ic * 512 + ocl, ic * 512 + ocl + 128),
                                  (ablk, ic * 512, (ic + 1) * 512)],
                           writes=[(pp, 0, 512)], tag="womm")
                fc = oc // 128
                P.emit("dve", lambda e, fc=fc, pp=pp: e.tensor_tensor(
                    rblk.h[:, fc * 512:(fc + 1) * 512],
                    rblk.h[:, fc * 512:(fc + 1) * 512],
                    pp.h[:, 0:512], ALU.add),
                    reads=[(pp, 0, 512), (rblk, fc * 512, (fc + 1) * 512)],
                    writes=[(rblk, fc * 512, (fc + 1) * 512)], tag="woadd")
        dma(P, "sp", (r1_d, 0, 1 << 58), lambda tb=tb:
            r1_d.h.rearrange("(fc p) t -> p fc t", p=128)[:, :, tb:tb + 512],
            (rblk, 0, FC * 512), lambda:
            rblk.h.ap().rearrange("p (fc t) -> p fc t", t=512))

    for tb in range(0, T_OWN, 512):
        dma(P, "sp", (rblk5, 0, FC * 512), lambda:
            rblk5.h.ap().rearrange("p (fc t) -> p fc t", t=512),
            (r1_d, 0, 1 << 58), lambda tb=tb:
            r1_d.h.rearrange("(fc p) t -> p fc t", p=128)[:, :, tb:tb + 512])
        ln_block(rblk5, 512, 512, sq5, 512, lambda fc: (xn1, fc * 512), None,
                 f"ln1_{tb}_")
        for ob in range(0, D_HID, 512):
            sb = ws5a if (ob // 512) % 2 == 0 else ws5b
            dma(P, "sp", (sb, 0, 1 << 58), lambda sb=sb:
                sb.h.ap().rearrange("p (ic o) -> p ic o", o=512),
                (upTs_d, 0, 1 << 58), lambda ob=ob:
                upTs_d.h.rearrange("(ic p) o -> p ic o", p=128)
                [:, :, ob:ob + 512], tag=f"up{ob}")
            for oc in range(ob, ob + 512, 128):
                ocl = oc - ob
                pp = ps[(oc // 128) % 2]
                for ic in range(FC):
                    P.emit("pe", lambda e, ic=ic, ocl=ocl, sb=sb, pp=pp:
                           e.matmul(pp.h[:, 0:512],
                                    sb.h[:, ic * 512 + ocl: ic * 512 + ocl + 128],
                                    xn1.h[:, ic * 512:(ic + 1) * 512],
                                    start=(ic == 0), stop=(ic == FC - 1)),
                           reads=[(sb, ic * 512 + ocl, ic * 512 + ocl + 128),
                                  (xn1, ic * 512, (ic + 1) * 512)],
                           writes=[(pp, 0, 512)], tag="upmm")
                hc = oc // 128
                P.emit("act", lambda e, hc=hc, pp=pp: e.activation(
                    hbuf.h[:, hc * 512:(hc + 1) * 512], pp.h[:, 0:512],
                    AF.Relu, scale=upsv.h[:, hc:hc + 1]),
                    reads=[(pp, 0, 512), upsv],
                    writes=[(hbuf, hc * 512, (hc + 1) * 512)], tag="uprelu")
        for oc in range(0, D, 128):
            sb = ws5a if (oc // 128) % 2 == 0 else ws5b
            dma(P, "sp", (sb, 0, 64 * 128), lambda sb=sb:
                sb.h.ap()[:, 0:64 * 128]
                .rearrange("p (ic o) -> p ic o", o=128),
                (dnTs_d, 0, 1 << 58), lambda oc=oc:
                dnTs_d.h.rearrange("(ic p) o -> p ic o", p=128)
                [:, :, oc:oc + 128], tag=f"dn{oc}")
            pp = ps[2 + (oc // 128) % 2]
            for ic in range(64):
                P.emit("pe", lambda e, ic=ic, sb=sb, pp=pp: e.matmul(
                    pp.h[:, 0:512], sb.h[:, ic * 128:(ic + 1) * 128],
                    hbuf.h[:, ic * 512:(ic + 1) * 512],
                    start=(ic == 0), stop=(ic == 63)),
                    reads=[(sb, ic * 128, (ic + 1) * 128),
                           (hbuf, ic * 512, (ic + 1) * 512)],
                    writes=[(pp, 0, 512)], tag="dnmm")
            fc = oc // 128
            P.emit("act", lambda e, fc=fc, pp=pp: e.activation(
                ev2.h[:, 0:512], pp.h[:, 0:512], AF.Identity,
                scale=dnscmb.h[:, fc:fc + 1]),
                reads=[(pp, 0, 512), dnscmb], writes=[(ev2, 0, 512)],
                tag="dnsc")
            P.emit("dve", lambda e, fc=fc: e.tensor_tensor(
                rblk5.h[:, fc * 512:(fc + 1) * 512],
                rblk5.h[:, fc * 512:(fc + 1) * 512], ev2.h[:, 0:512], ALU.add),
                reads=[(ev2, 0, 512), (rblk5, fc * 512, (fc + 1) * 512)],
                writes=[(rblk5, fc * 512, (fc + 1) * 512)], tag="dnadd")
        dma(P, "sp", (x2_o, 0, 1 << 58), lambda tb=tb:
            x2_o.h.rearrange("(fc p) t -> p fc t", p=128)[:, :, tb:tb + 512],
            (rblk5, 0, FC * 512), lambda:
            rblk5.h.ap().rearrange("p (fc t) -> p fc t", t=512))
        ln_block(rblk5, 512, 512, sq5, 512, lambda fc: (xn2b, fc * 512), None,
                 f"ln2_{tb}_")
        dma(P, "sp", (xn2_d, 0, 1 << 58), lambda tb=tb:
            xn2_d.h.rearrange("(fc p) t -> p fc t", p=128)[:, :, tb:tb + 512],
            (xn2b, 0, FC * 512), lambda:
            xn2b.h.ap().rearrange("p (fc t) -> p fc t", t=512))

    proj_fm("l1_wqT", xn2_d, 0, T_OWN, evac_to_dram(q1_o, bqv), tag="q1")
    proj_fm("l1_wkT", xn2_d, 0, T_OWN, evac_to_dram(k1_o, bkv), tag="k1")
    proj_tm("l1_wvT", xn2_d, 0, T_OWN, v1_o, bvrow, tag="v1")

    for h in range(N_HEADS):
        for qc in range(2):
            dma(P, "sp", (qh, 0, 512), lambda: qh.h[:],
                (q1_o, 0, 1 << 58), lambda h=h, qc=qc:
                q1_o.h[h * 128:(h + 1) * 128, qc * 512:(qc + 1) * 512])
            P.emit("act", lambda e: e.activation(sqv.h[:, 0:512], qh.h[:],
                                                 AF.Square),
                   reads=[(qh, 0, 512)], writes=[(sqv, 0, 512)])
            P.emit("pe", lambda e: e.matmul(ps_a.h[0:1, 0:512], ones_f.h[:],
                                            sqv.h[:, 0:512], start=True,
                                            stop=True),
                   reads=[ones_f, (sqv, 0, 512)], writes=[(ps_a, 0, 512)])
            P.emit("act", lambda e: e.activation(
                rows.h[0:1, 0:512], ps_a.h[0:1, 0:512], AF.Sqrt),
                reads=[(ps_a, 0, 512)], writes=[(rows, 0, 512)])
            off = h * 1024 + qc * 512
            dma(P, "sp", (qn1_o, 0, 1 << 58), lambda off=off:
                qn1_o.h[0:1, off:off + 512],
                (rows, 0, 512), lambda: rows.h[0:1, 0:512])
        for qc in range(2):
            dma(P, "sp", (qh, 0, 512), lambda: qh.h[:],
                (k1_o, 0, 1 << 58), lambda h=h, qc=qc:
                k1_o.h[h * 128:(h + 1) * 128, qc * 512:(qc + 1) * 512])
            P.emit("act", lambda e: e.activation(sqv.h[:, 0:512], qh.h[:],
                                                 AF.Square),
                   reads=[(qh, 0, 512)], writes=[(sqv, 0, 512)])
            P.emit("pe", lambda e: e.matmul(ps_a.h[0:1, 0:512], ones_f.h[:],
                                            sqv.h[:, 0:512], start=True,
                                            stop=True),
                   reads=[ones_f, (sqv, 0, 512)], writes=[(ps_a, 0, 512)])
            P.emit("dve", lambda e, h=h, qc=qc: e.reduce_max(
                knrow.h[0:1, h * 2 + qc:h * 2 + qc + 1],
                ps_a.h[0:1, 0:512], axis=AX.X),
                reads=[(ps_a, 0, 512)],
                writes=[(knrow, h * 2 + qc, h * 2 + qc + 1)])
        P.emit("dve", lambda e, h=h: e.tensor_tensor(
            knrow.h[0:1, h * 2:h * 2 + 1], knrow.h[0:1, h * 2:h * 2 + 1],
            knrow.h[0:1, h * 2 + 1:h * 2 + 2], ALU.max),
            reads=[(knrow, h * 2, h * 2 + 2)],
            writes=[(knrow, h * 2, h * 2 + 1)])
        P.emit("act", lambda e, h=h: e.activation(
            kn1r.h[0:1, h:h + 1], knrow.h[0:1, h * 2:h * 2 + 1], AF.Sqrt),
            reads=[(knrow, h * 2, h * 2 + 1)], writes=[(kn1r, h, h + 1)])
    dma(P, "sp", (kn1_o, 0, 1 << 58), lambda: kn1_o.h[:],
        (kn1r, 0, 1 << 58), lambda: kn1r.h[:])

    return P



def _fmT(a):
    return np.ascontiguousarray(np.asarray(a, np.float32).T)


def _vec16(v):
    return np.ascontiguousarray(np.asarray(v, np.float32).reshape(-1, 128).T)


def _l0_masks(core):
    own = core * T_OWN
    m = np.zeros((2, 6, 128, 512), np.float32)
    j = np.arange(512)[None, :]
    i = np.arange(128)[:, None]
    for qc in range(2):
        q = own + qc * 512 + j
        for r in range(6):
            k = own - HALO + qc * 512 + r * 128 + i
            ok = (k <= q) & (k >= q - (WINDOW - 1)) & (k >= 0)
            m[qc, r] = np.where(ok, np.float32(0.0), np.float32(NEG))
    return m


def _build_l1_inputs(inputs, core):
    x0 = np.asarray(inputs["L0_attn_x"][0], np.float32)
    x0f = _fmT(x0)
    own = core * T_OWN
    sl = np.zeros((D, TW), np.float32)
    lo = max(0, own - HALO)
    sl[:, HALO - (own - lo):] = x0f[:, lo:own + T_OWN]
    return {
        "x0s": sl,
        "gfm": _fmT(np.asarray(inputs["L0_attn_global_tokens"][0], np.float32)),
        "l0_wqT": _fmT(inputs["L0_attn_Wq"]),
        "l0_wkT": _fmT(inputs["L0_attn_Wk"]),
        "l0_wvT": _fmT(inputs["L0_attn_Wv"]),
        "l0_woT": _fmT(inputs["L0_attn_Wo"]),
        "l0_upT": _fmT(inputs["L0_ffn_up_weight"]),
        "l0_downT": _fmT(inputs["L0_ffn_down_weight"]),
        "ln_s": _vec16(inputs["L0_attn_ln_scale"]),
        "ln_b": _vec16(inputs["L0_attn_ln_bias"]),
        "ups": _vec16(inputs["L0_ffn_up_scale"]),
        "dns": _vec16(inputs["L0_ffn_down_scale"]),
        "oscal": np.asarray(inputs["L0_ffn_output_scale"],
                            np.float32).reshape(1, 1),
        "l1_wqT": _fmT(inputs["L1_attn_wq"]),
        "l1_wkT": _fmT(inputs["L1_attn_wk"]),
        "l1_wvT": _fmT(inputs["L1_attn_wv"]),
        "bq16": _vec16(inputs["L1_attn_bq"]),
        "bk16": _vec16(inputs["L1_attn_bk"]),
        "bv16": _vec16(inputs["L1_attn_bv"]),
        "m0": _l0_masks(core),
        "identf": np.eye(128, dtype=np.float32),
    }


def _run(nc, in_maps):
    res = run_bass_kernel_spmd(nc, in_maps, core_ids=list(range(N_CORES)))
    return res.results





def build_launch2(debug=False, stop=None, nheads=N_HEADS, nkt=64):
    P = Prog("l2")
    NKT = nkt

    x2_i = P.dram("x2_i", [D, T_OWN], F32, "ExternalInput")
    q1_i = P.dram("q1_i", [D, T_OWN], BF16, "ExternalInput")
    krot = P.dram("krot", [2, D, T_FULL], BF16, "ExternalInput")
    vrot = P.dram("vrot", [2, N_HEADS, 128, T_FULL], BF16, "ExternalInput")
    qn1_i = P.dram("qn1_i", [1, N_HEADS * T_OWN], F32, "ExternalInput")
    knall = P.dram("knall", [1, 128], F32, "ExternalInput")
    flagcol = P.dram("flagcol", [128, 128], F32, "ExternalInput")
    dmask = P.dram("dmask", [4, 128, 512], F32, "ExternalInput")
    bo = P.dram("bo16", [128, FC], F32, "ExternalInput")
    woT = P.dram("l1_woT", [D, D], F32, "ExternalInput")
    wrt = P.dram("w_router", [D, N_EXP], F32, "ExternalInput")
    w1 = P.dram("w1", [N_EXP, D, F_EXP], F32, "ExternalInput")
    w2 = P.dram("w2", [N_EXP, F_EXP, D], F32, "ExternalInput")
    w3 = P.dram("w3", [N_EXP, D, F_EXP], F32, "ExternalInput")
    ws1 = P.dram("ws1", [D, FS], F32, "ExternalInput")
    ws2 = P.dram("ws2", [FS, D], F32, "ExternalInput")
    ws3 = P.dram("ws3", [D, FS], F32, "ExternalInput")
    iota8 = P.dram("iota8", [128, N_EXP], F32, "ExternalInput")

    out_o = P.dram("out_o", [D, T_OWN], F32, "ExternalOutput")
    kd = "ExternalOutput" if debug else "Internal"
    r3_d = P.dram("r3_d", [D, T_OWN], F32, kd)
    gates_d = P.dram("gates_d", [N_EXP, T_OWN], F32, kd)
    at1_d = P.dram("at1_d", [D, T_OWN], BF16, kd) if debug else None
    xn3_dbg = P.dram("xn3_dbg", [D, T_OWN], BF16, kd) if debug else None

    AR = P.sbuf("AR", 128, 92 * 1024, BF16)
    KB = 1024

    q1f = P.view("q1f", 128, FC * T_OWN, BF16, AR, 0)
    kh = [P.view(f"kh{i}", 128, T_FULL, BF16, AR, (32 + 16 * i) * KB)
          for i in range(2)]
    vh = [P.view(f"vh{i}", 128, T_FULL, BF16, AR, (64 + 16 * i) * KB)
          for i in range(2)]
    attn = P.view("attn", 128, FC * T_OWN, BF16, AR, 96 * KB)
    dmk = P.view("dmk", 128, 4 * 512, F32, AR, 128 * KB)
    ptb = [P.view(f"ptb{i}", 128, 512, BF16, AR, (136 + i) * KB)
           for i in range(2)]
    rblk = P.view("rblk", 128, FC * 512, F32, AR, 0)
    wsa = P.view("wsa", 128, FC * 512, BF16, AR, 32 * KB)
    wsb = P.view("wsb", 128, FC * 512, BF16, AR, 48 * KB)
    xn3 = P.view("xn3", 128, FC * T_OWN, BF16, AR, 64 * KB)
    sqL = P.view("sqL", 128, 8 * 512, F32, AR, 128 * KB)
    macc = P.view("macc", 128, FC * T_OWN, F32, AR, 0)
    wea = P.view("wea", 128, FC * 512, BF16, AR, 96 * KB)
    web = P.view("web", 128, FC * 512, BF16, AR, 112 * KB)
    hg2 = P.view("hg2", 128, 11 * 1024, BF16, AR, 160 * KB)
    silu_t = P.view("silu_t", 128, 512, BF16, AR, 182 * KB)
    wec = P.view("wec", 128, FC * 512, BF16, AR, 128 * KB)
    wed = P.view("wed", 128, FC * 512, BF16, AR, 144 * KB)
    hgs = P.view("hgs", 128, 22 * 512, BF16, AR, 160 * KB)
    rfin = P.view("rfin", 128, FC * 512, F32, AR, 96 * KB)

    ev = P.sbuf("ev", 128, 512, BF16)
    ev2 = P.sbuf("ev2", 128, 512, F32)
    bc2 = P.sbuf("bc2", 128, 512, F32)
    gb = P.sbuf("gb", 128, 512, F32)
    rows = P.sbuf("rows", 1, 3072, F32)
    rowbf = P.sbuf("rowbf", 1, 512, BF16)
    ones_f = P.sbuf("ones_f", 128, 1, F32)
    ones_b = P.sbuf("ones_b", 128, 1, BF16)
    onesr_f = P.sbuf("onesr_f", 1, 128, F32)
    onesr_b = P.sbuf("onesr_b", 1, 128, BF16)
    eps1 = P.sbuf("eps1", 1, 1, F32)
    bov = P.sbuf("bov", 128, FC, F32)
    fcl_sb = P.sbuf("fcl_sb", 128, 128, F32)
    kn_sb = P.sbuf("kn_sb", 1, 128, F32)
    knm = P.sbuf("knm", 1, N_HEADS, F32)
    wr_sb = P.sbuf("wr_sb", 128, FC * N_EXP, BF16)
    io8 = P.sbuf("io8", 128, N_EXP, F32)
    mif = P.sbuf("mif", 128, N_EXP, F32)
    s8 = P.sbuf("s8", 128, N_EXP, F32)
    mm8 = P.sbuf("mm8", 128, N_EXP, F32)
    mi8 = P.sbuf("mi8", 128, N_EXP, U32)
    msk0 = P.sbuf("msk0", 128, N_EXP, F32)
    msk1 = P.sbuf("msk1", 128, N_EXP, F32)
    g12 = P.sbuf("g12", 128, 4, F32)
    call = P.sbuf("call", 128, N_EXP, F32)
    idf_sb = P.sbuf("idf_sb", 128, 128, F32)
    idf = P.dram("identf", [128, 128], F32, "ExternalInput")

    ps = [P.psum(f"ps{i}", 128, 512) for i in range(6)]
    ps_a = P.psum("ps_a", 128, 512)
    ps_b = P.psum("ps_b", 128, 512)

    P.emit("dve", lambda e: e.memset(eps1.h[:], LN_EPS), writes=[eps1])
    P.emit("dve", lambda e: e.memset(ones_f.h[:], 1.0), writes=[ones_f])
    P.emit("dve", lambda e: e.memset(ones_b.h[:], 1.0), writes=[ones_b])
    P.emit("dve", lambda e: e.memset(onesr_f.h[:], 1.0), writes=[onesr_f])
    P.emit("dve", lambda e: e.memset(onesr_b.h[:], 1.0), writes=[onesr_b])
    for (buf, src) in [(bov, bo), (fcl_sb, flagcol), (kn_sb, knall),
                       (io8, iota8), (idf_sb, idf)]:
        dma(P, "sp", (buf, 0, 1 << 58), lambda b=buf: b.h[:],
            (src, 0, 1 << 58), lambda s=src: s.h[:])
    dma(P, "sp", (dmk, 0, 1 << 58), lambda:
        dmk.h.ap().rearrange("p (k t) -> p k t", t=512),
        (dmask, 0, 1 << 58), lambda: dmask.h.rearrange("k p t -> p k t"))
    dma(P, "pool", (wr_sb, 0, 1 << 58), lambda:
        wr_sb.h.ap().rearrange("p (ic o) -> p ic o", o=N_EXP),
        (wrt, 0, 1 << 58), lambda:
        wrt.h.rearrange("(ic p) o -> p ic o", p=128))
    dma(P, "sp", (q1f, 0, 1 << 58), lambda:
        q1f.h.ap().rearrange("p (fc t) -> p fc t", t=T_OWN),
        (q1_i, 0, 1 << 58), lambda:
        q1_i.h.rearrange("(fc p) t -> p fc t", p=128))
    for h in range(N_HEADS):
        P.emit("dve", lambda e, h=h: e.reduce_max(
            knm.h[0:1, h:h + 1], kn_sb.h[0:1, h * 8:(h + 1) * 8], axis=AX.X),
            reads=[(kn_sb, h * 8, (h + 1) * 8)], writes=[(knm, h, h + 1)])

    QN, NB, RR = 1536, 2048, 2560

    def l1_attn(h, qc, step):
        kb, vb = kh[step % 2], vh[step % 2]
        dma(P, "sp", (kb, 0, T_FULL), lambda kb=kb: kb.h[:],
            (krot, 0, 1 << 58), lambda h=h, qc=qc:
            krot.h[qc, h * 128:(h + 1) * 128, :])
        dma(P, "sp", (vb, 0, T_FULL), lambda vb=vb: vb.h[:],
            (vrot, 0, 1 << 58), lambda h=h, qc=qc: vrot.h[qc, h])
        dma(P, "act", (rows, QN, QN + 512), lambda:
            rows.h[0:1, QN:QN + 512],
            (qn1_i, 0, 1 << 58), lambda h=h, qc=qc:
            qn1_i.h[0:1, h * T_OWN + qc * 512: h * T_OWN + qc * 512 + 512])
        P.emit("dve", lambda e, h=h: e.tensor_scalar(
            rows.h[0:1, NB:NB + 512], rows.h[0:1, QN:QN + 512],
            knm.h[0:1, h:h + 1], -1.0, ALU.mult, ALU.mult),
            reads=[(rows, QN, QN + 512), (knm, h, h + 1)],
            writes=[(rows, NB, NB + 512)])
        P.emit("act", lambda e: e.activation(rowbf.h[0:1, 0:512],
                                             rows.h[0:1, NB:NB + 512],
                                             AF.Identity),
               reads=[(rows, NB, NB + 512)], writes=[(rowbf, 0, 512)])
        qoff = h * T_OWN + qc * 512

        def emit_st(kt):
            pp = ps[kt % 2]
            P.emit("pe", lambda e, kt=kt, pp=pp, kb=kb, qoff=qoff: e.matmul(
                pp.h[:, 0:512], kb.h[:, kt * 128:(kt + 1) * 128],
                q1f.h[:, qoff:qoff + 512], start=True, stop=False),
                reads=[(kb, kt * 128, (kt + 1) * 128), (q1f, qoff, qoff + 512)],
                writes=[(pp, 0, 512)], tag="l1st")
            P.emit("pe", lambda e, pp=pp: e.matmul(
                pp.h[:, 0:512], onesr_b.h[:], rowbf.h[0:1, 0:512],
                start=False, stop=True),
                reads=[onesr_b, (rowbf, 0, 512)], writes=[(pp, 0, 512)])

        def emit_tail(kt):
            pp = ps[kt % 2]
            if kt >= 60:
                P.emit("dve", lambda e, pp=pp, kt=kt: e.tensor_tensor(
                    pp.h[:, 0:512], pp.h[:, 0:512],
                    dmk.h[:, (kt - 60) * 512:(kt - 59) * 512], ALU.add),
                    reads=[(pp, 0, 512),
                           (dmk, (kt - 60) * 512, (kt - 59) * 512)],
                    writes=[(pp, 0, 512)], tag="l1dm")
            pb = ptb[kt % 2]
            P.emit("act", lambda e, pp=pp, pb=pb: e.activation(
                pb.h[:], pp.h[:, 0:512], AF.Exp, scale=SCALE_ATT),
                reads=[(pp, 0, 512)], writes=[pb], tag="l1exp")
            fidx = qc * 64 + kt
            P.emit("dve", lambda e, pb=pb, fidx=fidx: e.tensor_scalar(
                pb.h[:], pb.h[:], fcl_sb.h[:, fidx:fidx + 1], None, ALU.mult),
                reads=[pb, (fcl_sb, fidx, fidx + 1)], writes=[pb], tag="l1fl")
            P.emit("pe", lambda e, kt=kt, pb=pb: e.matmul(
                ps[2].h[0:1, 0:512], ones_b.h[:], pb.h[:],
                start=(kt == 0), stop=(kt == NKT - 1)),
                reads=[ones_b, pb], writes=[(ps[2], 0, 512)], tag="l1sum")
            P.emit("pe", lambda e, kt=kt, pb=pb, vb=vb: e.matmul(
                ps[3].h[:, 0:512], vb.h[:, kt * 128:(kt + 1) * 128], pb.h[:],
                start=(kt == 0), stop=(kt == NKT - 1)),
                reads=[(vb, kt * 128, (kt + 1) * 128), pb],
                writes=[(ps[3], 0, 512)], tag="l1pv")

        emit_st(0)
        for kt in range(1, NKT):
            emit_st(kt)
            emit_tail(kt - 1)
        emit_tail(NKT - 1)
        P.emit("dve", lambda e: e.reciprocal(rows.h[0:1, RR:RR + 512],
                                             ps[2].h[0:1, 0:512]),
               reads=[(ps[2], 0, 512)], writes=[(rows, RR, RR + 512)])
        P.emit("pe", lambda e: e.matmul(ps[4].h[:, 0:512], onesr_f.h[:],
                                        rows.h[0:1, RR:RR + 512],
                                        start=True, stop=True),
               reads=[onesr_f, (rows, RR, RR + 512)],
               writes=[(ps[4], 0, 512)])
        P.emit("dve", lambda e: e.tensor_copy(gb.h[:], ps[4].h[:, 0:512]),
               reads=[(ps[4], 0, 512)], writes=[gb])
        P.emit("dve", lambda e, qoff=qoff: e.tensor_tensor(
            attn.h[:, qoff:qoff + 512], ps[3].h[:, 0:512], gb.h[:], ALU.mult),
            reads=[(ps[3], 0, 512), gb], writes=[(attn, qoff, qoff + 512)],
            tag="l1nrm")

    step = 0
    for h in range(nheads):
        for qc in range(2):
            l1_attn(h, qc, step)
            step += 1

    if debug:
        dma(P, "sp", (at1_d, 0, 1 << 58), lambda:
            at1_d.h.rearrange("(fc p) t -> p fc t", p=128),
            (attn, 0, 1 << 58), lambda:
            attn.h.ap().rearrange("p (fc t) -> p fc t", t=T_OWN))

    if stop == "attn":
        return P
    def ln_block2(src, TB, SRCW, sq, SQW, dst, tag):
        nT = TB
        for half in range(2):
            f0 = half * 8
            P.emit("act", lambda e, f0=f0: e.activation(
                sq.h.ap().rearrange("p (fc t) -> p fc t", t=SQW)[:, :, 0:nT],
                src.h.ap()[:, 0:FC * SRCW]
                .rearrange("p (fc t) -> p fc t", t=SRCW)[:, f0:f0 + 8, 0:nT],
                AF.Square),
                reads=[(src, f0 * SRCW, (f0 + 8) * SRCW)],
                writes=[(sq, 0, 8 * SQW)], tag=tag + "sq")
            for fc in range(8):
                fcg = f0 + fc
                P.emit("pe", lambda e, fc=fc, fcg=fcg: e.matmul(
                    ps_b.h[0:1, 0:nT], ones_f.h[:],
                    sq.h[:, fc * SQW: fc * SQW + nT],
                    start=(fcg == 0), stop=(fcg == FC - 1)),
                    reads=[ones_f, (sq, fc * SQW, fc * SQW + nT)],
                    writes=[(ps_b, 0, nT)], tag=tag + "ssq")
        for fc in range(FC):
            P.emit("pe", lambda e, fc=fc: e.matmul(
                ps_a.h[0:1, 0:nT], ones_f.h[:],
                src.h[:, fc * SRCW: fc * SRCW + nT],
                start=(fc == 0), stop=(fc == FC - 1)),
                reads=[ones_f, (src, fc * SRCW, fc * SRCW + nT)],
                writes=[(ps_a, 0, nT)], tag=tag + "ssum")
        MU, RST, NMU = 0, 512, 1024
        P.emit("dve", lambda e: e.tensor_single_scalar(
            rows.h[0:1, MU:MU + nT], ps_a.h[0:1, 0:nT], 1.0 / D, ALU.mult),
            reads=[(ps_a, 0, nT)], writes=[(rows, MU, MU + nT)])
        P.emit("dve", lambda e: e.tensor_single_scalar(
            rows.h[0:1, RST:RST + nT], ps_b.h[0:1, 0:nT], 1.0 / D, ALU.mult),
            reads=[(ps_b, 0, nT)], writes=[(rows, RST, RST + nT)])
        P.emit("dve", lambda e: e.tensor_tensor(
            rows.h[0:1, NMU:NMU + nT], rows.h[0:1, MU:MU + nT],
            rows.h[0:1, MU:MU + nT], ALU.mult),
            reads=[(rows, MU, MU + nT)], writes=[(rows, NMU, NMU + nT)])
        P.emit("dve", lambda e: e.tensor_tensor(
            rows.h[0:1, RST:RST + nT], rows.h[0:1, RST:RST + nT],
            rows.h[0:1, NMU:NMU + nT], ALU.subtract),
            reads=[(rows, RST, RST + nT), (rows, NMU, NMU + nT)],
            writes=[(rows, RST, RST + nT)])
        P.emit("act", lambda e: e.activation(
            rows.h[0:1, RST:RST + nT], rows.h[0:1, RST:RST + nT], AF.Sqrt,
            bias=eps1.h[:]), reads=[(rows, RST, RST + nT), eps1],
            writes=[(rows, RST, RST + nT)])
        P.emit("dve", lambda e: e.reciprocal(
            rows.h[0:1, RST:RST + nT], rows.h[0:1, RST:RST + nT]),
            reads=[(rows, RST, RST + nT)], writes=[(rows, RST, RST + nT)])
        P.emit("dve", lambda e: e.scalar_tensor_tensor(
            rows.h[0:1, NMU:NMU + nT], rows.h[0:1, MU:MU + nT], -1.0,
            rows.h[0:1, RST:RST + nT], ALU.mult, ALU.mult),
            reads=[(rows, MU, MU + nT), (rows, RST, RST + nT)],
            writes=[(rows, NMU, NMU + nT)])
        P.emit("pe", lambda e: e.matmul(
            ps_a.h[:, 0:nT], onesr_f.h[:], rows.h[0:1, RST:RST + nT],
            start=True, stop=True),
            reads=[onesr_f, (rows, RST, RST + nT)], writes=[(ps_a, 0, nT)])
        P.emit("pe", lambda e: e.matmul(
            ps_b.h[:, 0:nT], onesr_f.h[:], rows.h[0:1, NMU:NMU + nT],
            start=True, stop=True),
            reads=[onesr_f, (rows, NMU, NMU + nT)], writes=[(ps_b, 0, nT)])
        P.emit("dve", lambda e: e.tensor_copy(ev2.h[:, 0:nT], ps_a.h[:, 0:nT]),
               reads=[(ps_a, 0, nT)], writes=[(ev2, 0, nT)])
        P.emit("dve", lambda e: e.tensor_copy(bc2.h[:, 0:nT], ps_b.h[:, 0:nT]),
               reads=[(ps_b, 0, nT)], writes=[(bc2, 0, nT)])
        for fc in range(FC):
            dtgt, dlo = dst(fc)
            P.emit("dve", lambda e, fc=fc: e.tensor_tensor(
                sq.h[:, 0:nT], src.h[:, fc * SRCW:fc * SRCW + nT],
                ev2.h[:, 0:nT], ALU.mult),
                reads=[(src, fc * SRCW, fc * SRCW + nT), (ev2, 0, nT)],
                writes=[(sq, 0, nT)], tag=tag + "ap1")
            P.emit("dve", lambda e, dtgt=dtgt, dlo=dlo: e.tensor_tensor(
                dtgt.h[:, dlo:dlo + nT], sq.h[:, 0:nT],
                bc2.h[:, 0:nT], ALU.add),
                reads=[(sq, 0, nT), (bc2, 0, nT)],
                writes=[(dtgt, dlo, dlo + nT)], tag=tag + "ap2")

    for tb in range(0, T_OWN, 512):
        dma(P, "sp", (rblk, 0, FC * 512), lambda:
            rblk.h.ap().rearrange("p (fc t) -> p fc t", t=512),
            (x2_i, 0, 1 << 58), lambda tb=tb:
            x2_i.h.rearrange("(fc p) t -> p fc t", p=128)[:, :, tb:tb + 512])
        for ob in range(0, D, 512):
            sb = wsa if (ob // 512) % 2 == 0 else wsb
            dma(P, "pool", (sb, 0, 1 << 58), lambda sb=sb:
                sb.h.ap().rearrange("p (ic o) -> p ic o", o=512),
                (woT, 0, 1 << 58), lambda ob=ob:
                woT.h.rearrange("(ic p) o -> p ic o", p=128)[:, :, ob:ob + 512],
                tag=f"wo{ob}")
            for oc in range(ob, ob + 512, 128):
                ocl = oc - ob
                pp = ps[(oc // 128) % 2]
                for ic in range(FC):
                    P.emit("pe", lambda e, ic=ic, ocl=ocl, sb=sb, pp=pp, tb=tb:
                           e.matmul(pp.h[:, 0:512],
                                    sb.h[:, ic * 512 + ocl: ic * 512 + ocl + 128],
                                    attn.h[:, ic * T_OWN + tb: ic * T_OWN + tb + 512],
                                    start=(ic == 0), stop=(ic == FC - 1)),
                           reads=[(sb, ic * 512 + ocl, ic * 512 + ocl + 128),
                                  (attn, ic * T_OWN + tb, ic * T_OWN + tb + 512)],
                           writes=[(pp, 0, 512)], tag="womm")
                fc = oc // 128
                P.emit("act", lambda e, fc=fc, pp=pp: e.activation(
                    ev2.h[:, 0:512], pp.h[:, 0:512], AF.Identity,
                    bias=bov.h[:, fc:fc + 1]),
                    reads=[(pp, 0, 512), bov], writes=[(ev2, 0, 512)])
                P.emit("dve", lambda e, fc=fc: e.tensor_tensor(
                    rblk.h[:, fc * 512:(fc + 1) * 512],
                    rblk.h[:, fc * 512:(fc + 1) * 512],
                    ev2.h[:, 0:512], ALU.add),
                    reads=[(ev2, 0, 512), (rblk, fc * 512, (fc + 1) * 512)],
                    writes=[(rblk, fc * 512, (fc + 1) * 512)], tag="woadd")
        dma(P, "sp", (r3_d, 0, 1 << 58), lambda tb=tb:
            r3_d.h.rearrange("(fc p) t -> p fc t", p=128)[:, :, tb:tb + 512],
            (rblk, 0, FC * 512), lambda:
            rblk.h.ap().rearrange("p (fc t) -> p fc t", t=512))
        ln_block2(rblk, 512, 512, sqL, 512,
                  lambda fc, tb=tb: (xn3, fc * T_OWN + tb), f"ln3_{tb}_")

    if debug:
        dma(P, "sp", (xn3_dbg, 0, 1 << 58), lambda:
            xn3_dbg.h.rearrange("(fc p) t -> p fc t", p=128),
            (xn3, 0, 1 << 58), lambda:
            xn3.h.ap().rearrange("p (fc t) -> p fc t", t=T_OWN))

    if stop == "pb":
        return P
    for tt in range(0, T_OWN, 128):
        pp = ps[5]
        for ic in range(FC):
            P.emit("pe", lambda e, ic=ic, tt=tt, pp=pp: e.matmul(
                pp.h[:, 0:N_EXP],
                xn3.h[:, ic * T_OWN + tt: ic * T_OWN + tt + 128],
                wr_sb.h[:, ic * N_EXP:(ic + 1) * N_EXP],
                start=(ic == 0), stop=(ic == FC - 1)),
                reads=[(xn3, ic * T_OWN + tt, ic * T_OWN + tt + 128),
                       (wr_sb, ic * N_EXP, (ic + 1) * N_EXP)],
                writes=[(pp, 0, N_EXP)], tag="rtmm")
        P.emit("act", lambda e, pp=pp: e.activation(
            s8.h[:], pp.h[:, 0:N_EXP], AF.Sigmoid),
            reads=[(pp, 0, N_EXP)], writes=[s8], tag="rtsg")
        P.emit("dve", lambda e: e.max(mm8.h[:], s8.h[:]),
               reads=[s8], writes=[mm8])
        P.emit("dve", lambda e: e.max_index(mi8.h[:], mm8.h[:], s8.h[:]),
               reads=[mm8, s8], writes=[mi8])
        P.emit("dve", lambda e: e.tensor_tensor(
            g12.h[:, 0:1], mm8.h[:, 0:1], mm8.h[:, 1:2], ALU.add),
            reads=[mm8], writes=[(g12, 0, 1)])
        P.emit("dve", lambda e: e.reciprocal(g12.h[:, 1:2], g12.h[:, 0:1]),
               reads=[(g12, 0, 1)], writes=[(g12, 1, 2)])
        P.emit("dve", lambda e: e.tensor_tensor(
            g12.h[:, 2:3], mm8.h[:, 0:1], g12.h[:, 1:2], ALU.mult),
            reads=[mm8, (g12, 1, 2)], writes=[(g12, 2, 3)])
        P.emit("dve", lambda e: e.tensor_tensor(
            g12.h[:, 3:4], mm8.h[:, 1:2], g12.h[:, 1:2], ALU.mult),
            reads=[mm8, (g12, 1, 2)], writes=[(g12, 3, 4)])
        P.emit("dve", lambda e: e.tensor_copy(mif.h[:], mi8.h[:]),
               reads=[mi8], writes=[mif])
        P.emit("dve", lambda e: e.tensor_scalar(
            msk0.h[:], io8.h[:], mif.h[:, 0:1], None, ALU.is_equal),
            reads=[io8, mif], writes=[msk0])
        P.emit("dve", lambda e: e.tensor_scalar(
            msk1.h[:], io8.h[:], mif.h[:, 1:2], None, ALU.is_equal),
            reads=[io8, mif], writes=[msk1])
        P.emit("dve", lambda e: e.tensor_scalar(
            msk0.h[:], msk0.h[:], g12.h[:, 2:3], None, ALU.mult),
            reads=[msk0, (g12, 2, 3)], writes=[msk0])
        P.emit("dve", lambda e: e.tensor_scalar(
            msk1.h[:], msk1.h[:], g12.h[:, 3:4], None, ALU.mult),
            reads=[msk1, (g12, 3, 4)], writes=[msk1])
        P.emit("dve", lambda e: e.tensor_tensor(
            call.h[:], msk0.h[:], msk1.h[:], ALU.add),
            reads=[msk0, msk1], writes=[call])
        P.emit("pe", lambda e: e.transpose(
            ps_a.h[0:N_EXP, 0:128], call.h[:], idf_sb.h[:]),
            reads=[call, idf_sb], writes=[(ps_a, 0, 128)])
        P.emit("dve", lambda e: e.tensor_copy(ev2.h[0:N_EXP, 0:128],
                                              ps_a.h[0:N_EXP, 0:128]),
               reads=[(ps_a, 0, 128)], writes=[(ev2, 0, 128)])
        dma(P, "act", (gates_d, 0, 1 << 58), lambda tt=tt:
            gates_d.h[:, tt:tt + 128],
            (ev2, 0, 128), lambda: ev2.h[0:N_EXP, 0:128])

    if stop == "router":
        return P
    P.emit("dve", lambda e: e.memset(macc.h[:], 0.0), writes=[macc])

    FEc = F_EXP // 128
    for ex in range(N_EXP):
        for tb in range(0, T_OWN, 512):
            dma(P, "act", (rows, 0, 512), lambda: rows.h[0:1, 0:512],
                (gates_d, 0, 1 << 58), lambda ex=ex, tb=tb:
                gates_d.h[ex:ex + 1, tb:tb + 512])
            P.emit("pe", lambda e: e.matmul(
                ps[4].h[:, 0:512], onesr_f.h[:], rows.h[0:1, 0:512],
                start=True, stop=True),
                reads=[onesr_f, (rows, 0, 512)], writes=[(ps[4], 0, 512)])
            P.emit("dve", lambda e: e.tensor_copy(gb.h[:], ps[4].h[:, 0:512]),
                   reads=[(ps[4], 0, 512)], writes=[gb], tag="gbc")
            for fs in range(3):
                f0 = fs * 512
                fw = min(512, F_EXP - f0)
                nfc = fw // 128
                w1b = wea if fs % 2 == 0 else wec
                w3b = web if fs % 2 == 0 else wed
                dma(P, "pool", (w1b, 0, 1 << 58), lambda fw=fw, w1b=w1b:
                    w1b.h.ap().rearrange("p (ic o) -> p ic o", o=512)
                    [:, :, 0:fw],
                    (w1, 0, 1 << 58), lambda ex=ex, f0=f0, fw=fw:
                    w1.h[ex].rearrange("(ic p) o -> p ic o", p=128)
                    [:, :, f0:f0 + fw], tag=f"w1_{ex}_{fs}")
                dma(P, "pool", (w3b, 0, 1 << 58), lambda fw=fw, w3b=w3b:
                    w3b.h.ap().rearrange("p (ic o) -> p ic o", o=512)
                    [:, :, 0:fw],
                    (w3, 0, 1 << 58), lambda ex=ex, f0=f0, fw=fw:
                    w3.h[ex].rearrange("(ic p) o -> p ic o", p=128)
                    [:, :, f0:f0 + fw], tag=f"w3_{ex}_{fs}")
                for fcl in range(nfc):
                    ocl = fcl * 128
                    pp = ps[fcl % 2]
                    pq = ps[2 + fcl % 2]
                    for ic in range(FC):
                        P.emit("pe", lambda e, ic=ic, ocl=ocl, tb=tb, pp=pp, w1b=w1b:
                               e.matmul(pp.h[:, 0:512],
                                        w1b.h[:, ic * 512 + ocl: ic * 512 + ocl + 128],
                                        xn3.h[:, ic * T_OWN + tb: ic * T_OWN + tb + 512],
                                        start=(ic == 0), stop=(ic == FC - 1)),
                               reads=[(w1b, ic * 512 + ocl, ic * 512 + ocl + 128),
                                      (xn3, ic * T_OWN + tb, ic * T_OWN + tb + 512)],
                               writes=[(pp, 0, 512)], tag="h1mm")
                    P.emit("act", lambda e, pp=pp: e.activation(
                        silu_t.h[:], pp.h[:, 0:512], AF.Silu),
                        reads=[(pp, 0, 512)], writes=[silu_t], tag="silu")
                    for ic in range(FC):
                        P.emit("pe", lambda e, ic=ic, ocl=ocl, tb=tb, pq=pq, w3b=w3b:
                               e.matmul(pq.h[:, 0:512],
                                        w3b.h[:, ic * 512 + ocl: ic * 512 + ocl + 128],
                                        xn3.h[:, ic * T_OWN + tb: ic * T_OWN + tb + 512],
                                        start=(ic == 0), stop=(ic == FC - 1)),
                               reads=[(w3b, ic * 512 + ocl, ic * 512 + ocl + 128),
                                      (xn3, ic * T_OWN + tb, ic * T_OWN + tb + 512)],
                               writes=[(pq, 0, 512)], tag="h3mm")
                    fcg = fs * 4 + fcl
                    P.emit("dve", lambda e, pq=pq: e.tensor_tensor(
                        ev.h[:, 0:512], silu_t.h[:], pq.h[:, 0:512], ALU.mult),
                        reads=[silu_t, (pq, 0, 512)],
                        writes=[(ev, 0, 512)], tag="hgm1")
                    hoff = fcg * T_OWN + tb
                    P.emit("dve", lambda e, hoff=hoff: e.tensor_tensor(
                        hg2.h[:, hoff:hoff + 512], ev.h[:, 0:512],
                        gb.h[:], ALU.mult),
                        reads=[(ev, 0, 512), gb],
                        writes=[(hg2, hoff, hoff + 512)], tag="hgm2")
        for oc in range(0, D, 128):
            sb = wea if (oc // 128) % 2 == 0 else web
            dma(P, "pool", (sb, 0, FEc * 128), lambda sb=sb:
                sb.h.ap()[:, 0:FEc * 128]
                .rearrange("p (ic o) -> p ic o", o=128),
                (w2, 0, 1 << 58), lambda ex=ex, oc=oc:
                w2.h[ex].rearrange("(ic p) o -> p ic o", p=128)
                [:, :, oc:oc + 128], tag=f"w2_{ex}_{oc}")
            for tb in range(0, T_OWN, 512):
                pp = ps[4 + (tb // 512) % 2]
                for ic in range(FEc):
                    P.emit("pe", lambda e, ic=ic, tb=tb, sb=sb, pp=pp:
                           e.matmul(pp.h[:, 0:512],
                                    sb.h[:, ic * 128:(ic + 1) * 128],
                                    hg2.h[:, ic * T_OWN + tb: ic * T_OWN + tb + 512],
                                    start=(ic == 0), stop=(ic == FEc - 1)),
                           reads=[(sb, ic * 128, (ic + 1) * 128),
                                  (hg2, ic * T_OWN + tb, ic * T_OWN + tb + 512)],
                           writes=[(pp, 0, 512)], tag="w2mm")
                fc = oc // 128
                moff = fc * T_OWN + tb
                P.emit("dve", lambda e, moff=moff, pp=pp: e.tensor_tensor(
                    macc.h[:, moff:moff + 512], macc.h[:, moff:moff + 512],
                    pp.h[:, 0:512], ALU.add),
                    reads=[(pp, 0, 512), (macc, moff, moff + 512)],
                    writes=[(macc, moff, moff + 512)], tag="w2add")

    if stop == "experts":
        return P
    FSc = FS // 128
    for tb in range(0, T_OWN, 512):
        for fs in range(6):
            f0 = fs * 512
            fw = min(512, FS - f0)
            nfc = fw // 128
            w1b = wea if fs % 2 == 0 else wec
            w3b = web if fs % 2 == 0 else wed
            dma(P, "pool", (w1b, 0, 1 << 58), lambda fw=fw, w1b=w1b:
                w1b.h.ap().rearrange("p (ic o) -> p ic o", o=512)[:, :, 0:fw],
                (ws1, 0, 1 << 58), lambda f0=f0, fw=fw:
                ws1.h.rearrange("(ic p) o -> p ic o", p=128)[:, :, f0:f0 + fw],
                tag=f"ws1_{fs}")
            dma(P, "pool", (w3b, 0, 1 << 58), lambda fw=fw, w3b=w3b:
                w3b.h.ap().rearrange("p (ic o) -> p ic o", o=512)[:, :, 0:fw],
                (ws3, 0, 1 << 58), lambda f0=f0, fw=fw:
                ws3.h.rearrange("(ic p) o -> p ic o", p=128)[:, :, f0:f0 + fw],
                tag=f"ws3_{fs}")
            for fcl in range(nfc):
                ocl = fcl * 128
                pp = ps[fcl % 2]
                for ic in range(FC):
                    P.emit("pe", lambda e, ic=ic, ocl=ocl, tb=tb, pp=pp, w1b=w1b:
                           e.matmul(pp.h[:, 0:512],
                                    w1b.h[:, ic * 512 + ocl: ic * 512 + ocl + 128],
                                    xn3.h[:, ic * T_OWN + tb: ic * T_OWN + tb + 512],
                                    start=(ic == 0), stop=(ic == FC - 1)),
                           reads=[(w1b, ic * 512 + ocl, ic * 512 + ocl + 128),
                                  (xn3, ic * T_OWN + tb, ic * T_OWN + tb + 512)],
                           writes=[(pp, 0, 512)], tag="s1mm")
                    P.emit("pe", lambda e, ic=ic, ocl=ocl, tb=tb, pp=pp, w3b=w3b:
                           e.matmul(ps[2].h[:, 0:512],
                                    w3b.h[:, ic * 512 + ocl: ic * 512 + ocl + 128],
                                    xn3.h[:, ic * T_OWN + tb: ic * T_OWN + tb + 512],
                                    start=(ic == 0), stop=(ic == FC - 1)),
                           reads=[(w3b, ic * 512 + ocl, ic * 512 + ocl + 128),
                                  (xn3, ic * T_OWN + tb, ic * T_OWN + tb + 512)],
                           writes=[(ps[2], 0, 512)], tag="s3mm")
                P.emit("act", lambda e, pp=pp: e.activation(
                    silu_t.h[:], pp.h[:, 0:512], AF.Silu),
                    reads=[(pp, 0, 512)], writes=[silu_t], tag="ssilu")
                fcg = fs * 4 + fcl
                P.emit("dve", lambda e, fcg=fcg: e.tensor_tensor(
                    hgs.h[:, fcg * 512:(fcg + 1) * 512], silu_t.h[:],
                    ps[2].h[:, 0:512], ALU.mult),
                    reads=[silu_t, (ps[2], 0, 512)],
                    writes=[(hgs, fcg * 512, (fcg + 1) * 512)], tag="shg")
        for oc in range(0, D, 128):
            sb = wea if (oc // 128) % 2 == 0 else web
            dma(P, "pool", (sb, 0, FSc * 128), lambda sb=sb:
                sb.h.ap()[:, 0:FSc * 128]
                .rearrange("p (ic o) -> p ic o", o=128),
                (ws2, 0, 1 << 58), lambda oc=oc:
                ws2.h.rearrange("(ic p) o -> p ic o", p=128)[:, :, oc:oc + 128],
                tag=f"ws2_{oc}")
            pp = ps[4 + (oc // 128) % 2]
            for ic in range(FSc):
                P.emit("pe", lambda e, ic=ic, sb=sb, pp=pp: e.matmul(
                    pp.h[:, 0:512], sb.h[:, ic * 128:(ic + 1) * 128],
                    hgs.h[:, ic * 512:(ic + 1) * 512],
                    start=(ic == 0), stop=(ic == FSc - 1)),
                    reads=[(sb, ic * 128, (ic + 1) * 128),
                           (hgs, ic * 512, (ic + 1) * 512)],
                    writes=[(pp, 0, 512)], tag="ws2mm")
            fc = oc // 128
            moff = fc * T_OWN + tb
            P.emit("dve", lambda e, moff=moff, pp=pp: e.tensor_tensor(
                macc.h[:, moff:moff + 512], macc.h[:, moff:moff + 512],
                pp.h[:, 0:512], ALU.add),
                reads=[(pp, 0, 512), (macc, moff, moff + 512)],
                writes=[(macc, moff, moff + 512)], tag="ws2add")

    if stop == "shared":
        return P
    for tb in range(0, T_OWN, 512):
        dma(P, "sp", (rfin, 0, FC * 512), lambda:
            rfin.h.ap().rearrange("p (fc t) -> p fc t", t=512),
            (r3_d, 0, 1 << 58), lambda tb=tb:
            r3_d.h.rearrange("(fc p) t -> p fc t", p=128)[:, :, tb:tb + 512])
        for fc in range(FC):
            P.emit("dve", lambda e, fc=fc, tb=tb: e.tensor_tensor(
                rfin.h[:, fc * 512:(fc + 1) * 512],
                rfin.h[:, fc * 512:(fc + 1) * 512],
                macc.h[:, fc * T_OWN + tb: fc * T_OWN + tb + 512], ALU.add),
                reads=[(macc, fc * T_OWN + tb, fc * T_OWN + tb + 512),
                       (rfin, fc * 512, (fc + 1) * 512)],
                writes=[(rfin, fc * 512, (fc + 1) * 512)], tag="fin")
        dma(P, "sp", (out_o, 0, 1 << 58), lambda tb=tb:
            out_o.h.rearrange("(fc p) t -> p fc t", p=128)[:, :, tb:tb + 512],
            (rfin, 0, FC * 512), lambda:
            rfin.h.ap().rearrange("p (fc t) -> p fc t", t=512))

    return P


def _build_l2_inputs(inputs, core, res1):
    Kf = np.concatenate([np.asarray(r["k1_o"]) for r in res1], axis=1)
    Vf = np.concatenate([np.asarray(r["v1_o"]) for r in res1], axis=0)
    knall = np.zeros((1, 128), np.float32)
    for c2 in range(N_CORES):
        kn = np.asarray(res1[c2]["kn1_o"]).reshape(-1)
        for h in range(N_HEADS):
            knall[0, h * 8 + c2] = kn[h]
    krot = np.zeros((2, D, T_FULL), BFNP)
    vrot = np.zeros((2, N_HEADS, 128, T_FULL), BFNP)
    flagc = np.zeros((128, 128), np.float32)
    for qc in range(2):
        gqc = 2 * core + qc
        roll = 7680 - gqc * 512
        krot[qc] = np.roll(Kf, roll, axis=1)
        vr = np.roll(Vf, roll, axis=0)
        vrot[qc] = (vr.reshape(64, 128, 16, 128)
                    .transpose(2, 1, 0, 3).reshape(16, 128, 64 * 128))
        for kt in range(64):
            flagc[:, qc * 64 + kt] = 1.0 if kt >= 60 - 4 * gqc else 0.0
    dmsk = np.zeros((4, 128, 512), np.float32)
    i = np.arange(128)[:, None]
    j = np.arange(512)[None, :]
    for k in range(4):
        dmsk[k] = np.where(k * 128 + i > j, np.float32(NEG), np.float32(0.0))
    return {
        "x2_i": np.asarray(res1[core]["x2_o"]),
        "q1_i": np.asarray(res1[core]["q1_o"]),
        "krot": krot,
        "vrot": vrot,
        "qn1_i": np.asarray(res1[core]["qn1_o"]),
        "knall": knall,
        "flagcol": flagc,
        "dmask": dmsk,
        "bo16": _vec16(inputs["L1_attn_bo"]),
        "l1_woT": _fmT(inputs["L1_attn_wo"]),
        "w_router": np.ascontiguousarray(np.asarray(inputs["L1_ffn_w_router"],
                                                    np.float32)),
        "w1": np.ascontiguousarray(np.asarray(inputs["L1_ffn_w1"], np.float32)),
        "w2": np.ascontiguousarray(np.asarray(inputs["L1_ffn_w2"], np.float32)),
        "w3": np.ascontiguousarray(np.asarray(inputs["L1_ffn_w3"], np.float32)),
        "ws1": np.ascontiguousarray(np.asarray(inputs["L1_ffn_ws1"], np.float32)),
        "ws2": np.ascontiguousarray(np.asarray(inputs["L1_ffn_ws2"], np.float32)),
        "ws3": np.ascontiguousarray(np.asarray(inputs["L1_ffn_ws3"], np.float32)),
        "iota8": np.broadcast_to(np.arange(8, dtype=np.float32)[None, :],
                                 (128, 8)).copy(),
        "identf": np.eye(128, dtype=np.float32),
    }


def kernel(**inputs):
    P1 = build_launch1()
    nc1 = bass.Bass("TRN2", target_bir_lowering=False, debug=False)
    P1.build(nc1)
    maps1 = [_build_l1_inputs(inputs, c) for c in range(N_CORES)]
    res1 = _run(nc1, maps1)

    P2 = build_launch2()
    nc2 = bass.Bass("TRN2", target_bir_lowering=False, debug=False)
    P2.build(nc2)
    maps2 = [_build_l2_inputs(inputs, c, res1) for c in range(N_CORES)]
    res2 = _run(nc2, maps2)

    full = np.concatenate([r["out_o"] for r in res2], axis=1)
    return np.ascontiguousarray(full.T)[None].astype(np.float32)
